# version 07
# speedup vs baseline: 1.2630x; 1.1767x over previous
import contextlib
import numpy as np

import concourse.bass as bass
import concourse.mybir as mybir
from concourse.bass_utils import run_bass_kernel_spmd

F32 = mybir.dt.float32
AF = mybir.ActivationFunctionType
ALU = mybir.AluOpType
AX = mybir.AxisListType

D = 1024
CCH = 128
HQ, HKV, HK, HV = 16, 4, 64, 64
F = 4096
E = 8
R = 16
SCALE = 2.0
EPS = 1e-6
LN_EPS = 1e-5
NEG = -1e30
CONST_VALS = (1e-5, 1e-6, 1.0, -1.0, 1.0 / 64, 0.0, 2.0, 0.5)
FS = 256
NS = F // FS


class Prog:

    NSYNC = 4
    ENGINES = ("tensor", "vector", "scalar", "gpsimd",
               "sync0", "sync1", "sync2", "sync3")

    def __init__(self):
        self.ops = []
        self.count = {e: 0 for e in self.ENGINES}
        self.last_w = {}
        self.last_r = {}
        self.inc = {"tensor": 1, "vector": 1, "scalar": 1, "gpsimd": 1,
                    "sync0": 16, "sync1": 16, "sync2": 16, "sync3": 16}
        self.barrier_floor = {}
        self.barrier_done = set()
        self.sync_rr = 0

    def op(self, eng, fn, reads=(), writes=()):
        if eng == "sync":
            eng = f"sync{self.sync_rr % self.NSYNC}"
            self.sync_rr += 1
        deps = {}

        def dep(e2, c2):
            if c2 > 0:
                deps[e2] = max(deps.get(e2, 0), c2)

        if eng not in self.barrier_done:
            for e2, c2 in self.barrier_floor.items():
                dep(e2, c2)
            self.barrier_done.add(eng)
        for b in reads:
            w = self.last_w.get(b)
            if w:
                dep(*w)
        for b in writes:
            w = self.last_w.get(b)
            if w:
                dep(*w)
            for e2, c2 in self.last_r.get(b, {}).items():
                dep(e2, c2)
        self.count[eng] += 1
        c = self.count[eng]
        self.ops.append((eng, fn, deps))
        for b in reads:
            self.last_r.setdefault(b, {})[eng] = c
        for b in writes:
            self.last_w[b] = (eng, c)
            self.last_r[b] = {}

    def dma(self, queue, out, in_, reads=(), writes=()):
        self.op(queue, lambda q: q.dma_start(out=out, in_=in_), reads, writes)

    def new_phase(self):
        self.barrier_floor = dict(self.count)
        self.barrier_done = set()
        marker = len(self.ops)
        return marker

    def emit(self, block, sems, start=0, closer=False):
        per_eng = {e: [] for e in self.ENGINES}
        idx = {e: 0 for e in self.ENGINES}
        for i, (eng, fn, deps) in enumerate(self.ops):
            idx[eng] += 1
            if i >= start:
                per_eng[eng].append((fn, deps, idx[eng]))

        def mk(names):
            merged = []
            for nm in names:
                merged.extend((nm, fn, deps, c) for (fn, deps, c) in per_eng[nm])
            if len(names) > 1:
                order = {}
                for i, (e3, fn, deps) in enumerate(self.ops):
                    order.setdefault(e3, []).append(i)
                pos = {nm: 0 for nm in names}
                seq = []
                for i, (e3, fn, deps) in enumerate(self.ops):
                    if i < start or e3 not in names:
                        continue
                    seq.append((e3, fn, deps, None))
                merged = []
                cnt = dict.fromkeys(names, 0)
                k = {nm: 0 for nm in names}
                for i, (e3, fn, deps) in enumerate(self.ops):
                    if e3 in names:
                        k[e3] += 1
                        if i >= start:
                            merged.append((e3, fn, deps, k[e3]))

            def body(eng):
                waited = {}
                for nm, fn, deps, c in merged:
                    if nm.startswith("sync") and c > 1:
                        tgt = (c - 1) * self.inc[nm]
                        if waited.get(nm, 0) < tgt:
                            eng.wait_ge(sems[nm], tgt)
                            waited[nm] = tgt
                    for e2, c2 in deps.items():
                        tgt = c2 * self.inc[e2]
                        if waited.get(e2, 0) < tgt:
                            eng.wait_ge(sems[e2], tgt)
                            waited[e2] = tgt
                    instr = fn(eng)
                    instr.then_inc(sems[nm], self.inc[nm])
            return body

        sync_names = [n for n in self.ENGINES if n.startswith("sync")]
        for name in self.ENGINES:
            if name.startswith("sync"):
                continue
            if per_eng[name]:
                getattr(block, name)(mk([name]))
        if any(per_eng[n] for n in sync_names):
            block.sync(mk(sync_names))
        if closer:
            final = {e: self.count[e] * self.inc[e] for e in self.ENGINES}

            def cl(eng):
                for e2, tgt in final.items():
                    if tgt:
                        eng.wait_ge(sems[e2], tgt)
            block.sync(cl)


def build_nc(Tloc=2048):
    TT_TOK = min(512, Tloc)
    NTT = Tloc // TT_TOK
    NCH = Tloc // CCH
    NSUB = TT_TOK // 128

    nc = bass.Bass("TRN2", target_bir_lowering=False, debug=False)
    dt = nc.dram_tensor

    x_in = dt("x", [Tloc, D], F32, kind="ExternalInput").ap()
    y_out = dt("y", [Tloc, D], F32, kind="ExternalOutput").ap()

    Wq = dt("Wq", [D, 1024], F32, kind="ExternalInput").ap()
    Wk = dt("Wk", [D, 256], F32, kind="ExternalInput").ap()
    Wv = dt("Wv", [D, 256], F32, kind="ExternalInput").ap()
    Wconv = dt("Wconv", [1536, 4], F32, kind="ExternalInput").ap()
    Wa = dt("Wa", [D, 16], F32, kind="ExternalInput").ap()
    Walpha = dt("Walpha", [D, 16], F32, kind="ExternalInput").ap()
    Wb = dt("Wb", [D, 4], F32, kind="ExternalInput").ap()
    A_log = dt("A_log", [16, 1], F32, kind="ExternalInput").ap()
    dt_bias = dt("dt_bias", [16, 1], F32, kind="ExternalInput").ap()
    norm_w2 = dt("norm_w2", [128, 1], F32, kind="ExternalInput").ap()
    Wo = dt("Wo", [1024, D], F32, kind="ExternalInput").ap()

    gate_w = dt("gate_w", [E, D], F32, kind="ExternalInput").ap()
    w1 = dt("w_gate", [F, D], F32, kind="ExternalInput").ap()
    w3 = dt("w_up", [F, D], F32, kind="ExternalInput").ap()
    wd = dt("w_down", [D, F], F32, kind="ExternalInput").ap()
    A1 = dt("A1", [E * R, D], F32, kind="ExternalInput").ap()
    B1 = dt("B1", [E * F, R], F32, kind="ExternalInput").ap()
    A3 = dt("A3", [E * R, D], F32, kind="ExternalInput").ap()
    B3 = dt("B3", [E * F, R], F32, kind="ExternalInput").ap()
    A2 = dt("A2", [E * R, F], F32, kind="ExternalInput").ap()
    B2 = dt("B2", [E * D, R], F32, kind="ExternalInput").ap()

    cvals_in = dt("cvals", [128, 8], F32, kind="ExternalInput").ap()
    ident_in = dt("ident", [128, 128], F32, kind="ExternalInput").ap()
    idsh_in = dt("idsh", [128, 128], F32, kind="ExternalInput").ap()
    mask_in = dt("maskc", [128, 128], F32, kind="ExternalInput").ap()
    iota8_in = dt("iota8", [128, 8], F32, kind="ExternalInput").ap()
    ones_in = dt("ones", [128, 128], F32, kind="ExternalInput").ap()

    w1T = dt("w1T", [D, F], F32, kind="Internal").ap()
    w3T = dt("w3T", [D, F], F32, kind="Internal").ap()
    wdT = dt("wdT", [F, D], F32, kind="Internal").ap()
    B1Tg = dt("B1Tg", [2 * 128, F], F32, kind="Internal").ap()
    B3Tg = dt("B3Tg", [2 * 128, F], F32, kind="Internal").ap()
    A2T = dt("A2T", [F, 256], F32, kind="Internal").ap()
    xprime = dt("xprime", [Tloc, D], F32, kind="Internal").ap()

    p = Prog()
    with contextlib.ExitStack() as st:
        E_ = st.enter_context
        sems = {e: E_(nc.semaphore(f"s_{e}")) for e in Prog.ENGINES}

        cvals = E_(nc.sbuf_tensor("cvals_sb", [128, 8], F32))
        ident = E_(nc.sbuf_tensor("ident_sb", [128, 128], F32))
        idsh = E_(nc.sbuf_tensor("idsh_sb", [128, 128], F32))
        maskc = E_(nc.sbuf_tensor("mask_sb", [128, 128], F32))
        iota8 = E_(nc.sbuf_tensor("iota8_sb", [128, 8], F32))
        ones = E_(nc.sbuf_tensor("ones_sb", [128, 128], F32))
        nw2 = E_(nc.sbuf_tensor("nw2_sb", [128, 1], F32))
        A1Tg = [E_(nc.sbuf_tensor(f"A1Tg{g}", [128, 8 * 128], F32)) for g in range(2)]
        A3Tg = [E_(nc.sbuf_tensor(f"A3Tg{g}", [128, 8 * 128], F32)) for g in range(2)]
        B2Tg = [E_(nc.sbuf_tensor(f"B2Tg{g}", [128, 1024], F32)) for g in range(2)]
        gwT = E_(nc.sbuf_tensor("gwT", [128, 8 * 8], F32))

        p.dma("sync", cvals[:], cvals_in, writes=["cvals"])
        for _i, _v in enumerate(CONST_VALS):
            nc.const_aps.register(F32, _v, cvals[:, _i:_i + 1])
        p.dma("sync", ident[:], ident_in, writes=["ident"])
        p.dma("sync", idsh[:], idsh_in, writes=["idsh"])
        p.dma("sync", maskc[:], mask_in, writes=["maskc"])
        p.dma("sync", iota8[:], iota8_in, writes=["iota8"])
        p.dma("sync", ones[:], ones_in, writes=["ones"])
        p.dma("sync", nw2[:], norm_w2, writes=["nw2"])

        p0_tasks = []

        def task(fn):
            p0_tasks.append(fn)

        with contextlib.ExitStack() as s0:
            E0 = s0.enter_context
            stgs = [E0(nc.sbuf_tensor(f"stg_{i}", [128, 1024], F32)) for i in range(2)]
            stg2s = [E0(nc.sbuf_tensor(f"stg2_{i}", [128, 1024], F32)) for i in range(1)]
            pts = [E0(nc.psum_tensor(f"pt{i}", [128, 512], F32)) for i in range(2)]

            def transpose_big(src_ap, dst_ap, rows, cols, tag):
                it = 0
                for r0 in range(0, rows, 128):
                    for cb0 in range(0, cols, 1024):
                        bi = it % 2
                        it += 1
                        def unit(bi=bi, r0=r0, cb0=cb0):
                            stg, stg2, pt = stgs[bi], stg2s[0], pts[bi]
                            sn, s2n, ptn = f"stg_{bi}", "stg2_0", f"pt{bi}"
                            w = min(1024, cols - cb0)
                            p.dma("sync", stg[:, 0:w], src_ap[r0:r0 + 128, cb0:cb0 + w],
                                  reads=[sn], writes=[sn])
                            for cb in range(w // 128):
                                def mkt(cb=cb, pt=pt, stg=stg):
                                    return lambda t: t.transpose(pt[:, 0:128], stg[:, cb * 128:(cb + 1) * 128], ident[:])
                                p.op("tensor", mkt(), reads=[sn, "ident"], writes=[ptn])
                                def mke(cb=cb, pt=pt, stg2=stg2):
                                    return lambda v: v.tensor_copy(stg2[:, cb * 128:(cb + 1) * 128], pt[:, 0:128])
                                p.op("vector", mke(), reads=[ptn], writes=[s2n])
                            for cb in range(w // 128):
                                p.dma("sync", dst_ap[cb0 + cb * 128: cb0 + (cb + 1) * 128, r0:r0 + 128],
                                      stg2[:, cb * 128:(cb + 1) * 128], reads=[s2n], writes=[f"d{tag}"])
                        task(unit)

            transpose_big(w1, w1T, F, D, "w1")
            transpose_big(w3, w3T, F, D, "w3")
            transpose_big(wd, wdT, D, F, "wd")

            for (srcA, dstA, tagA) in ((A1, A1Tg, "a1"), (A3, A3Tg, "a3")):
                for g in range(2):
                    def unit(dstA=dstA, g=g, tagA=tagA):
                        p.op("gpsimd", lambda gp, d=dstA[g]: gp.memset(d[:], 0.0),
                             writes=[f"AT{tagA}{g}"])
                    task(unit)
                for e in range(E):
                    g, el = divmod(e, 4)
                    bi = e % 2
                    def unit(bi=bi, e=e, g=g, el=el, srcA=srcA, dstA=dstA, tagA=tagA):
                        stg, pt = stgs[bi], pts[bi]
                        sn, ptn = f"stg_{bi}", f"pt{bi}"
                        p.dma("sync", stg[0:16, 0:D], srcA[e * R:(e + 1) * R, :],
                              reads=[sn], writes=[sn])
                        for dc in range(8):
                            def mkt(dc=dc, pt=pt, stg=stg):
                                return lambda t: t.transpose(pt[:, 0:16], stg[0:16, dc * 128:(dc + 1) * 128],
                                                             ident[0:16, 0:16])
                            p.op("tensor", mkt(), reads=[sn, "ident"], writes=[ptn])
                            def mke(dc=dc, dstT=dstA[g], el=el, pt=pt):
                                return lambda v: v.tensor_copy(
                                    dstT[:, dc * 128 + el * 32: dc * 128 + el * 32 + 16], pt[:, 0:16])
                            p.op("vector", mke(), reads=[ptn], writes=[f"AT{tagA}{g}"])
                    task(unit)

            for g in range(2):
                def unit(g=g):
                    p.op("gpsimd", lambda gp, d=B2Tg[g]: gp.memset(d[:], 0.0), writes=[f"B2T{g}"])
                task(unit)
            for e in range(E):
                g, el = divmod(e, 4)
                bi = e % 2
                def unit(bi=bi, e=e, g=g, el=el):
                    stg, pt = stgs[bi], pts[bi]
                    sn, ptn = f"stg_{bi}", f"pt{bi}"
                    p.dma("sync", stg[:, 0:128].rearrange("p (dc r) -> p dc r", r=16),
                          B2[e * D: (e + 1) * D, :].rearrange("(dc p) r -> p dc r", p=128),
                          reads=[sn], writes=[sn])
                    for dc in range(8):
                        def mkt(dc=dc, pt=pt, stg=stg):
                            return lambda t: t.transpose(pt[0:16, 0:128], stg[:, dc * 16:(dc + 1) * 16], ident[:])
                        p.op("tensor", mkt(), reads=[sn, "ident"], writes=[ptn])
                        def mke(dc=dc, g=g, el=el, pt=pt):
                            return lambda v: v.tensor_copy(
                                B2Tg[g][el * 32: el * 32 + 16, dc * 128:(dc + 1) * 128], pt[0:16, 0:128])
                        p.op("vector", mke(), reads=[ptn], writes=[f"B2T{g}"])
                task(unit)

            for (srcB, dstB, tagB) in ((B1, B1Tg, "b1"), (B3, B3Tg, "b3")):
                for g in range(2):
                    for fb in range(F // 1024):
                        def unit(srcB=srcB, dstB=dstB, tagB=tagB, g=g, fb=fb):
                            p.op("gpsimd", lambda gp: gp.memset(stg2s[0][:], 0.0),
                                 reads=["stg2_0"], writes=["stg2_0"])
                            for el in range(4):
                                e = g * 4 + el
                                f0 = fb * 1024
                                bi = el % 2
                                stg, pt = stgs[bi], pts[bi]
                                sn, ptn = f"stg_{bi}", f"pt{bi}"
                                p.dma("sync", stg[:, 0:128].rearrange("p (fc r) -> p fc r", r=16),
                                      srcB[e * F + f0: e * F + f0 + 1024, :].rearrange("(fc p) r -> p fc r", p=128),
                                      reads=[sn], writes=[sn])
                                for fc in range(8):
                                    def mkt(fc=fc, pt=pt, stg=stg):
                                        return lambda t: t.transpose(pt[0:16, 0:128],
                                                                     stg[:, fc * 16:(fc + 1) * 16], ident[:])
                                    p.op("tensor", mkt(), reads=[sn, "ident"], writes=[ptn])
                                    def mke(el=el, fc=fc, pt=pt):
                                        return lambda v: v.tensor_copy(
                                            stg2s[0][el * 32: el * 32 + 16, fc * 128:(fc + 1) * 128], pt[0:16, 0:128])
                                    p.op("vector", mke(), reads=[ptn], writes=["stg2_0"])
                            p.dma("sync", dstB[g * 128:(g + 1) * 128, fb * 1024:(fb + 1) * 1024], stg2s[0][:],
                                  reads=["stg2_0"], writes=[f"dB{tagB}"])
                        task(unit)

            for fq in range(8):
                def unit(fq=fq):
                    p.op("gpsimd", lambda gp: gp.memset(stg2s[0][:, 0:1024], 0.0),
                         reads=["stg2_0"], writes=["stg2_0"])
                    for e in range(E):
                        g, el = divmod(e, 4)
                        bi = e % 2
                        stg, pt = stgs[bi], pts[bi]
                        sn, ptn = f"stg_{bi}", f"pt{bi}"
                        p.dma("sync", stg[0:16, 0:512], A2[e * R:(e + 1) * R, fq * 512:(fq + 1) * 512],
                              reads=[sn], writes=[sn])
                        for fc4 in range(4):
                            def mkt(fc4=fc4, pt=pt, stg=stg):
                                return lambda t: t.transpose(pt[:, 0:16], stg[0:16, fc4 * 128:(fc4 + 1) * 128],
                                                             ident[0:16, 0:16])
                            p.op("tensor", mkt(), reads=[sn, "ident"], writes=[ptn])
                            def mke(g=g, el=el, fc4=fc4, pt=pt):
                                return lambda v: v.tensor_copy(
                                    stg2s[0][:, fc4 * 256 + g * 128 + el * 32: fc4 * 256 + g * 128 + el * 32 + 16],
                                    pt[:, 0:16])
                            p.op("vector", mke(), reads=[ptn], writes=["stg2_0"])
                    for fc4 in range(4):
                        fc = fq * 4 + fc4
                        p.dma("sync", A2T[fc * 128:(fc + 1) * 128, :], stg2s[0][:, fc4 * 256:(fc4 + 1) * 256],
                              reads=["stg2_0"], writes=["dA2"])
                task(unit)
            def unit_gate():
                p.dma("sync", stgs[0][0:8, 0:D], gate_w, reads=["stg_0"], writes=["stg_0"])
                for dc in range(8):
                    def mkt(dc=dc):
                        return lambda t: t.transpose(pts[0][:, 0:8], stgs[0][0:8, dc * 128:(dc + 1) * 128],
                                                     ident[0:8, 0:8])
                    p.op("tensor", mkt(), reads=["stg_0", "ident"], writes=["pt0"])
                    def mke(dc=dc):
                        return lambda v: v.tensor_copy(gwT[:, dc * 8:(dc + 1) * 8], pts[0][:, 0:8])
                    p.op("vector", mke(), reads=["pt0"], writes=["gwT"])
            task(unit_gate)

            EA = s0.enter_context
            WqS = EA(nc.sbuf_tensor("WqS", [128, 8 * 1024], F32))
            WkS = EA(nc.sbuf_tensor("WkS", [128, 8 * 256], F32))
            WvS = EA(nc.sbuf_tensor("WvS", [128, 8 * 256], F32))
            WoS = EA(nc.sbuf_tensor("WoS", [128, 8 * 1024], F32))
            WaS = EA(nc.sbuf_tensor("WaS", [128, 8 * 16], F32))
            WalS = EA(nc.sbuf_tensor("WalS", [128, 8 * 16], F32))
            WbS = EA(nc.sbuf_tensor("WbS", [128, 8 * 4], F32))
            WcS = EA(nc.sbuf_tensor("WcS", [128, 12 * 4], F32))
            AlogS = EA(nc.sbuf_tensor("AlogS", [16, 1], F32))
            dtbS = EA(nc.sbuf_tensor("dtbS", [16, 1], F32))
            negeA = EA(nc.sbuf_tensor("negeA", [16, 1], F32))
            xc = [EA(nc.sbuf_tensor(f"xc{i}", [128, 1024], F32)) for i in range(2)]
            xn_tm = EA(nc.sbuf_tensor("xn_tm", [128, 1024], F32))
            xn_fm = EA(nc.sbuf_tensor("xn_fm", [128, 8 * 128], F32))
            ring = EA(nc.sbuf_tensor("ring", [128, 12 * 131], F32))
            qkvs = EA(nc.sbuf_tensor("qkvs", [128, 12 * 128], F32))
            cacc = EA(nc.sbuf_tensor("cacc", [128, 2 * 128], F32))
            g_a = EA(nc.sbuf_tensor("g_a", [16, 128], F32))
            g_b = EA(nc.sbuf_tensor("g_b", [16, 128], F32))
            g_t1 = EA(nc.sbuf_tensor("g_t1", [16, 128], F32))
            g_t2 = EA(nc.sbuf_tensor("g_t2", [16, 128], F32))
            gamT = EA(nc.sbuf_tensor("gamT", [128, 16], F32))
            apreT = EA(nc.sbuf_tensor("apreT", [128, 16], F32))
            bpreT = EA(nc.sbuf_tensor("bpreT", [128, 4], F32))
            rowd = EA(nc.sbuf_tensor("rowd", [1, 256], F32))
            dmt_all = EA(nc.sbuf_tensor("dmt_all", [128, 16 * 128], F32))
            lamb_all = EA(nc.sbuf_tensor("lamb_all", [128, 16 * 128], F32))
            dms = EA(nc.sbuf_tensor("dms", [128, 256], F32))
            ms2 = EA(nc.sbuf_tensor("ms2", [2, 1024], F32))
            ap2 = EA(nc.sbuf_tensor("ap2", [2, 1024], F32))
            a2q = EA(nc.sbuf_tensor("a2q", [2, 1024], F32))
            r2 = EA(nc.sbuf_tensor("r2", [2, 256], F32))
            r4T = EA(nc.sbuf_tensor("r4T", [128, 4], F32))
            bsig = EA(nc.sbuf_tensor("bsig", [4, 128], F32))
            bsigT = EA(nc.sbuf_tensor("bsigT", [128, 4], F32))
            uT = EA(nc.sbuf_tensor("uT", [128, 16], F32))
            ones2 = EA(nc.sbuf_tensor("ones2", [128, 2], F32))
            vdup = EA(nc.sbuf_tensor("vdup", [64, 128], F32))
            kfm = [EA(nc.sbuf_tensor(f"kfm{i}", [128, 128], F32)) for i in range(4)]
            ktm = EA(nc.sbuf_tensor("ktm", [128, 4 * 64], F32))
            vtm = EA(nc.sbuf_tensor("vtm", [128, 4 * 64], F32))
            kwv = EA(nc.sbuf_tensor("kwv", [128, 128], F32))
            skk = EA(nc.sbuf_tensor("skk", [128, 8 * 64], F32))
            skv = EA(nc.sbuf_tensor("skv", [128, 8 * 64], F32))
            qs_b = EA(nc.sbuf_tensor("qs_b", [128, 128], F32))
            x1_b = EA(nc.sbuf_tensor("x1_b", [128, 128], F32))
            x2_b = EA(nc.sbuf_tensor("x2_b", [128, 128], F32))
            tt_b = EA(nc.sbuf_tensor("tt_b", [128, 128], F32))
            uu_b = EA(nc.sbuf_tensor("uu_b", [128, 128], F32))
            s_tmp = EA(nc.sbuf_tensor("s_tmp", [128, 128], F32))
            ocol = EA(nc.sbuf_tensor("ocol", [128, 8 * 128], F32))
            aout = EA(nc.sbuf_tensor("aout", [128, 1024], F32))
            sqs = EA(nc.sbuf_tensor("sqs", [128, 1024], F32))
            cols = EA(nc.sbuf_tensor("colsA", [128, 16], F32))
            rows = EA(nc.sbuf_tensor("rowsA", [1, 128], F32))
            P0 = EA(nc.psum_tensor("P0", [128, 512], F32))
            P1 = EA(nc.psum_tensor("P1", [128, 512], F32))
            P2 = EA(nc.psum_tensor("P2", [128, 512], F32))
            P3 = EA(nc.psum_tensor("P3", [128, 512], F32))
            P4 = EA(nc.psum_tensor("P4", [128, 512], F32))
            P5 = EA(nc.psum_tensor("P5", [128, 512], F32))
            blkA = EA(nc.Block())

            def fm_load(dst, src, name):
                c = src.shape[0] // 128
                o = src.shape[1]
                p.dma("sync", dst[:].rearrange("p (c o) -> p c o", c=c),
                      src.rearrange("(c p) o -> p c o", p=128), writes=[name])

            fm_load(WqS, Wq, "WqS")
            fm_load(WkS, Wk, "WkS")
            fm_load(WvS, Wv, "WvS")
            fm_load(WoS, Wo, "WoS")
            fm_load(WaS, Wa, "WaS")
            fm_load(WalS, Walpha, "WalS")
            fm_load(WbS, Wb, "WbS")
            fm_load(WcS, Wconv, "WcS")
            p.dma("sync", AlogS[:], A_log, writes=["AlogS"])
            p.dma("sync", dtbS[:], dt_bias, writes=["dtbS"])
            p.op("scalar", lambda s: s.activation(negeA[:], AlogS[:], AF.Exp),
                 reads=["AlogS"], writes=["negeA"])
            p.op("vector", lambda v: v.tensor_scalar(out=negeA[:], in0=negeA[:], scalar1=-1.0,
                                                     scalar2=None, op0=ALU.mult),
                 reads=["negeA"], writes=["negeA"])
            p.op("gpsimd", lambda g: g.memset(skk[:], 0.0), writes=["skk"])
            p.op("gpsimd", lambda g: g.memset(skv[:], 0.0), writes=["skv"])
            p.op("gpsimd", lambda g: g.memset(ring[:], 0.0), writes=["ring"])
            p.op("gpsimd", lambda g: g.memset(ones2[:], 0.0), writes=["ones2"])
            p.op("gpsimd", lambda g: g.memset(ones2[0:64, 0:1], 1.0), reads=["ones2"], writes=["ones2"])
            p.op("gpsimd", lambda g: g.memset(ones2[64:128, 1:2], 1.0), reads=["ones2"], writes=["ones2"])

            def ln_inline(src_buf, src_ap, dst_ap, dst_w, c0, sq, sqn, colb, coln):
                p.op("scalar", lambda s: s.activation(sq[:, 0:1024], src_ap, AF.Identity,
                                                      accum_out=colb[:, c0:c0 + 1]),
                     reads=[src_buf], writes=[sqn, coln])
                p.op("scalar", lambda s: s.activation(sq[:, 0:1024], src_ap, AF.Square,
                                                      accum_out=colb[:, c0 + 1:c0 + 2]),
                     reads=[src_buf], writes=[sqn, coln])
                p.op("vector", lambda v: v.tensor_scalar(out=colb[:, c0:c0 + 1], in0=colb[:, c0:c0 + 1],
                                                         scalar1=1.0 / D, scalar2=None, op0=ALU.mult),
                     reads=[coln], writes=[coln])
                p.op("scalar", lambda s: s.activation(colb[:, c0 + 2:c0 + 3], colb[:, c0:c0 + 1], AF.Square),
                     reads=[coln], writes=[coln])
                p.op("vector", lambda v: v.scalar_tensor_tensor(out=colb[:, c0 + 3:c0 + 4],
                                                                in0=colb[:, c0 + 1:c0 + 2],
                                                                scalar=1.0 / D, in1=colb[:, c0 + 2:c0 + 3],
                                                                op0=ALU.mult, op1=ALU.subtract),
                     reads=[coln], writes=[coln])
                p.op("scalar", lambda s: s.activation(colb[:, c0 + 4:c0 + 5], colb[:, c0 + 3:c0 + 4],
                                                      AF.Sqrt, bias=LN_EPS),
                     reads=[coln], writes=[coln])
                p.op("vector", lambda v: v.reciprocal(colb[:, c0 + 4:c0 + 5], colb[:, c0 + 4:c0 + 5]),
                     reads=[coln], writes=[coln])
                p.op("vector", lambda v: v.tensor_scalar(out=dst_ap, in0=src_ap,
                                                         scalar1=colb[:, c0:c0 + 1],
                                                         scalar2=colb[:, c0 + 4:c0 + 5],
                                                         op0=ALU.subtract, op1=ALU.mult),
                     reads=[src_buf, coln], writes=[dst_w])

            ntask = (len(p0_tasks) + NCH - 1) // NCH
            for c in range(NCH):
                for tk in p0_tasks[c * ntask:(c + 1) * ntask]:
                    tk()
                xbuf = xc[c % 2]
                xnm = f"xc{c % 2}"
                p.dma("sync", xbuf[:], x_in[c * 128:(c + 1) * 128, :], reads=[xnm], writes=[xnm])
                ln_inline(xnm, xbuf[:], xn_tm[:], "xn_tm", 0, sqs, "sqs", cols, "colsA")
                for dc in range(8):
                    def mkt(dc=dc):
                        return lambda t: t.transpose(P5[:, 0:128], xn_tm[:, dc * 128:(dc + 1) * 128], ident[:])
                    p.op("tensor", mkt(), reads=["xn_tm", "ident"], writes=["P5"])
                    def mke(dc=dc):
                        return lambda v: v.tensor_copy(xn_fm[:, dc * 128:(dc + 1) * 128], P5[:, 0:128])
                    p.op("vector", mke(), reads=["P5"], writes=["xn_fm"])

                for t12 in range(12):
                    def mkc(t12=t12):
                        return lambda v: v.tensor_copy(ring[:, t12 * 131: t12 * 131 + 3],
                                                       ring[:, t12 * 131 + 128: t12 * 131 + 131])
                    p.op("vector", mkc(), reads=["ring"], writes=["ring"])

                mt = 0
                for (Wt, nmt, wn, wcols) in ((WqS, 8, "WqS", 1024), (WkS, 2, "WkS", 256), (WvS, 2, "WvS", 256)):
                    for m in range(nmt):
                        ps = (P0, P1, P2)[mt // 4]
                        psn = f"P{mt // 4}"
                        off = (mt % 4) * 128
                        for dc in range(8):
                            def mkm(Wt=Wt, m=m, dc=dc, ps=ps, off=off, wcols=wcols):
                                return lambda t: t.matmul(
                                    ps[:, off:off + 128],
                                    Wt[:, dc * wcols + m * 128: dc * wcols + m * 128 + 128],
                                    xn_fm[:, dc * 128:(dc + 1) * 128],
                                    start=(dc == 0), stop=(dc == 7))
                            p.op("tensor", mkm(), reads=[wn, "xn_fm"], writes=[psn])
                        mt += 1
                for t12 in range(12):
                    ps = (P0, P1, P2)[t12 // 4]
                    off = (t12 % 4) * 128
                    def mke(t12=t12, ps=ps, off=off):
                        return lambda v: v.tensor_copy(ring[:, t12 * 131 + 3: t12 * 131 + 131],
                                                       ps[:, off:off + 128])
                    p.op("vector", mke(), reads=[f"P{t12 // 4}"], writes=["ring"])

                for (Wt, wn, off, m) in ((WaS, "WaS", 0, 16), (WalS, "WalS", 128, 16), (WbS, "WbS", 256, 4)):
                    for dc in range(8):
                        def mkm(Wt=Wt, dc=dc, off=off, m=m):
                            return lambda t: t.matmul(P3[0:m, off:off + 128],
                                                      Wt[:, dc * m: (dc + 1) * m],
                                                      xn_fm[:, dc * 128:(dc + 1) * 128],
                                                      start=(dc == 0), stop=(dc == 7))
                        p.op("tensor", mkm(), reads=[wn, "xn_fm"], writes=["P3"])

                for t12 in range(12):
                    cb = cacc[:, (t12 % 2) * 128:(t12 % 2) * 128 + 128]
                    def mk0(t12=t12, cb=cb):
                        return lambda v: v.tensor_scalar(out=cb, in0=ring[:, t12 * 131: t12 * 131 + 128],
                                                         scalar1=WcS[:, t12 * 4: t12 * 4 + 1],
                                                         scalar2=None, op0=ALU.mult)
                    p.op("vector", mk0(), reads=["ring", "WcS"], writes=["cacc"])
                    for j in range(1, 4):
                        def mkj(t12=t12, j=j, cb=cb):
                            return lambda v: v.scalar_tensor_tensor(
                                out=cb, in0=ring[:, t12 * 131 + j: t12 * 131 + j + 128],
                                scalar=WcS[:, t12 * 4 + j: t12 * 4 + j + 1],
                                in1=cb, op0=ALU.mult, op1=ALU.add)
                        p.op("vector", mkj(), reads=["ring", "WcS", "cacc"], writes=["cacc"])
                    def mks(t12=t12, cb=cb):
                        return lambda s: s.activation(qkvs[:, t12 * 128:(t12 + 1) * 128], cb, AF.Silu)
                    p.op("scalar", mks(), reads=["cacc"], writes=["qkvs"])

                p.op("scalar", lambda s: s.activation(g_a[:], P3[0:16, 0:128], AF.Identity, bias=dtbS[:]),
                     reads=["P3", "dtbS"], writes=["g_a"])
                p.op("scalar", lambda s: s.activation(g_t1[:], g_a[:], AF.Abs),
                     reads=["g_a"], writes=["g_t1"])
                p.op("scalar", lambda s: s.activation(g_t1[:], g_t1[:], AF.Exp, scale=-1.0),
                     reads=["g_t1"], writes=["g_t1"])
                p.op("scalar", lambda s: s.activation(g_t1[:], g_t1[:], AF.Ln, bias=1.0),
                     reads=["g_t1"], writes=["g_t1"])
                p.op("scalar", lambda s: s.activation(g_t2[:], g_a[:], AF.Relu),
                     reads=["g_a"], writes=["g_t2"])
                p.op("vector", lambda v: v.tensor_tensor(out=g_a[:], in0=g_t1[:], in1=g_t2[:], op=ALU.add),
                     reads=["g_t1", "g_t2"], writes=["g_a"])
                p.op("vector", lambda v: v.tensor_scalar(out=g_a[:], in0=g_a[:], scalar1=negeA[:],
                                                         scalar2=None, op0=ALU.mult),
                     reads=["g_a", "negeA"], writes=["g_a"])
                src, srcn, dst, dstn = g_a, "g_a", g_b, "g_b"
                sh = 1
                while sh < 128:
                    def mkcp(src=src, dst=dst, sh=sh):
                        return lambda v: v.tensor_copy(dst[:, 0:sh], src[:, 0:sh])
                    p.op("vector", mkcp(), reads=[srcn], writes=[dstn])
                    def mka(src=src, dst=dst, sh=sh):
                        return lambda v: v.tensor_tensor(out=dst[:, sh:128], in0=src[:, sh:128],
                                                         in1=src[:, 0:128 - sh], op=ALU.add)
                    p.op("vector", mka(), reads=[srcn], writes=[dstn])
                    src, srcn, dst, dstn = dst, dstn, src, srcn
                    sh *= 2
                gam, gamn = src, srcn

                p.op("tensor", lambda t: t.transpose(P4[:, 0:16], gam[:], ident[0:16, 0:16]),
                     reads=[gamn, "ident"], writes=["P4"])
                p.op("vector", lambda v: v.tensor_copy(gamT[:], P4[:, 0:16]),
                     reads=["P4"], writes=["gamT"])
                p.op("vector", lambda v: v.tensor_copy(g_t1[:], P3[0:16, 128:256]),
                     reads=["P3"], writes=["g_t1"])
                p.op("vector", lambda v: v.tensor_copy(g_t2[0:4, :], P3[0:4, 256:384]),
                     reads=["P3"], writes=["g_t2"])
                p.op("tensor", lambda t: t.transpose(P4[:, 16:32], g_t1[:], ident[0:16, 0:16]),
                     reads=["g_t1", "ident"], writes=["P4"])
                p.op("vector", lambda v: v.tensor_copy(apreT[:], P4[:, 16:32]),
                     reads=["P4"], writes=["apreT"])
                p.op("tensor", lambda t: t.transpose(P4[:, 32:36], g_t2[0:4, :], ident[0:4, 0:4]),
                     reads=["g_t2", "ident"], writes=["P4"])
                p.op("vector", lambda v: v.tensor_copy(bpreT[:], P4[:, 32:36]),
                     reads=["P4"], writes=["bpreT"])

                p.op("scalar", lambda s: s.activation(sqs[:, 0:256], qkvs[:, 8 * 128:10 * 128], AF.Square),
                     reads=["qkvs"], writes=["sqs"])
                for t2 in range(2):
                    def mkssb(t2=t2):
                        return lambda t: t.matmul(P4[0:2, 128:256], ones2[:, 0:2],
                                                  sqs[:, t2 * 128:(t2 + 1) * 128],
                                                  start=True, stop=True)
                    p.op("tensor", mkssb(), reads=["sqs", "ones2"], writes=["P4"])
                    def mksse(t2=t2):
                        return lambda v: v.tensor_copy(r2[0:2, t2 * 128:(t2 + 1) * 128], P4[0:2, 128:256])
                    p.op("vector", mksse(), reads=["P4"], writes=["r2"])
                p.op("scalar", lambda s: s.activation(r2[:], r2[:], AF.Sqrt),
                     reads=["r2"], writes=["r2"])
                p.op("vector", lambda v: v.tensor_scalar(out=r2[:], in0=r2[:], scalar1=EPS,
                                                         scalar2=None, op0=ALU.add),
                     reads=["r2"], writes=["r2"])
                p.op("vector", lambda v: v.reciprocal(r2[:], r2[:]), reads=["r2"], writes=["r2"])
                for t2 in range(2):
                    def mkrt(t2=t2):
                        return lambda t: t.transpose(P4[:, 256:258], r2[0:2, t2 * 128:(t2 + 1) * 128],
                                                     ident[0:2, 0:2])
                    p.op("tensor", mkrt(), reads=["r2", "ident"], writes=["P4"])
                    def mkrte(t2=t2):
                        return lambda v: v.tensor_copy(r4T[:, t2 * 2:(t2 + 1) * 2], P4[:, 256:258])
                    p.op("vector", mkrte(), reads=["P4"], writes=["r4T"])
                p.op("tensor", lambda t: t.matmul(P5[0:4, 0:128], bpreT[:], ident[:],
                                                  start=True, stop=True),
                     reads=["bpreT", "ident"], writes=["P5"])
                p.op("scalar", lambda s: s.activation(bsig[:], P5[0:4, 0:128], AF.Sigmoid),
                     reads=["P5"], writes=["bsig"])
                p.op("tensor", lambda t: t.transpose(P5[:, 128:132], bsig[:], ident[0:4, 0:4]),
                     reads=["bsig", "ident"], writes=["P5"])
                p.op("vector", lambda v: v.tensor_copy(bsigT[:], P5[:, 128:132]),
                     reads=["P5"], writes=["bsigT"])

                for kv in range(4):
                    tile = 8 + kv // 2
                    b = 64 * (kv % 2)
                    ob = 64 - b
                    kpre = qkvs[b:b + 64, tile * 128:(tile + 1) * 128]
                    vpre = qkvs[b:b + 64, (tile + 2) * 128:(tile + 3) * 128]
                    def mkrx(kv=kv):
                        return lambda t: t.matmul(P4[0:1, 384:512], r4T[:, kv:kv + 1], ident[:],
                                                  start=True, stop=True)
                    p.op("tensor", mkrx(), reads=["r4T", "ident"], writes=["P4"])
                    p.op("scalar", lambda s: s.activation(rows[:, 0:128], P4[0:1, 384:512], AF.Identity),
                         reads=["P4"], writes=["rowsA"])
                    def mkbc(b=b):
                        tp = (0, b) if b else None
                        return lambda t: t.matmul(P4[b:b + 64, 256:384], ones[0:1, 0:64], rows[:, 0:128],
                                                  start=True, stop=True, tile_position=tp)
                    p.op("tensor", mkbc(), reads=["rowsA", "ones"], writes=["P4"])
                    def mkkn(kv=kv, b=b, kpre=kpre):
                        return lambda v: v.tensor_tensor(out=kfm[kv][b:b + 64, 0:128], in0=kpre,
                                                         in1=P4[b:b + 64, 256:384], op=ALU.mult)
                    p.op("vector", mkkn(), reads=["qkvs", "P4"], writes=[f"kfm{kv}"])
                    def mkdup2(kv=kv, b=b, ob=ob):
                        st_ap = ident[0:64, 0:64] if b == 0 else idsh[64:128, 0:64]
                        return lambda t: t.matmul(P4[ob:ob + 64, 384:512], st_ap,
                                                  kfm[kv][b:b + 64, 0:128],
                                                  start=True, stop=True, tile_position=(b, ob))
                    p.op("tensor", mkdup2(), reads=[f"kfm{kv}", "ident", "idsh"], writes=["P4"])
                    def mkdupe(kv=kv, ob=ob):
                        return lambda v: v.tensor_copy(kfm[kv][ob:ob + 64, 0:128], P4[ob:ob + 64, 384:512])
                    p.op("vector", mkdupe(), reads=["P4"], writes=[f"kfm{kv}"])
                    def mkbx(kv=kv):
                        return lambda t: t.matmul(P5[0:1, 384:512], bsigT[:, kv:kv + 1], ident[:],
                                                  start=True, stop=True)
                    p.op("tensor", mkbx(), reads=["bsigT", "ident"], writes=["P5"])
                    p.op("scalar", lambda s: s.activation(rowd[0:1, 0:128], P5[0:1, 384:512], AF.Identity),
                         reads=["P5"], writes=["rowd"])
                    def mkbb(b=b):
                        tp = (0, b) if b else None
                        return lambda t: t.matmul(P5[b:b + 64, 256:384], ones[0:1, 0:64], rowd[0:1, 0:128],
                                                  start=True, stop=True, tile_position=tp)
                    p.op("tensor", mkbb(), reads=["rowd", "ones"], writes=["P5"])
                    def mkvs(vpre=vpre, b=b):
                        return lambda v: v.tensor_tensor(out=vpre, in0=vpre, in1=P5[b:b + 64, 256:384],
                                                         op=ALU.mult)
                    p.op("vector", mkvs(), reads=["qkvs", "P5"], writes=["qkvs"])
                    if b == 0:
                        vsrc_ap = vpre
                        vsrc_n = "qkvs"
                    else:
                        def mkvd(vpre=vpre):
                            return lambda t: t.matmul(P4[0:64, 384:512], idsh[64:128, 0:64], vpre,
                                                      start=True, stop=True, tile_position=(64, 0))
                        p.op("tensor", mkvd(), reads=["qkvs", "idsh"], writes=["P4"])
                        p.op("vector", lambda v: v.tensor_copy(vdup[:], P4[0:64, 384:512]),
                             reads=["P4"], writes=["vdup"])
                        vsrc_ap = vdup[:]
                        vsrc_n = "vdup"
                    def mktk(kv=kv):
                        return lambda t: t.transpose(P5[:, 384:448], kfm[kv][0:64, 0:128], ident[0:64, 0:64])
                    p.op("tensor", mktk(), reads=[f"kfm{kv}", "ident"], writes=["P5"])
                    def mktke(kv=kv):
                        return lambda v: v.tensor_copy(ktm[:, kv * 64:(kv + 1) * 64], P5[:, 384:448])
                    p.op("vector", mktke(), reads=["P5"], writes=["ktm"])
                    def mktv(vsrc_ap=vsrc_ap):
                        return lambda t: t.transpose(P5[:, 448:512], vsrc_ap, ident[0:64, 0:64])
                    p.op("tensor", mktv(), reads=[vsrc_n, "ident"], writes=["P5"])
                    def mktve(kv=kv):
                        return lambda v: v.tensor_copy(vtm[:, kv * 64:(kv + 1) * 64], P5[:, 448:512])
                    p.op("vector", mktve(), reads=["P5"], writes=["vtm"])

                for h in range(16):
                    par = h % 2
                    bankB, bn = ((P0, "P0"), (P3, "P3"))[par]
                    def mkgx(h=h, bankB=bankB):
                        return lambda t: t.matmul(bankB[0:1, 128:256], gamT[:, h:h + 1], ident[:],
                                                  start=True, stop=True)
                    p.op("tensor", mkgx(), reads=["gamT", "ident"], writes=[bn])
                    def mkgc(par=par, bankB=bankB):
                        return lambda s: s.activation(rowd[0:1, par * 128:(par + 1) * 128],
                                                      bankB[0:1, 128:256], AF.Identity)
                    p.op("scalar", mkgc(), reads=[bn], writes=["rowd"])
                    def mkgb(par=par, bankB=bankB):
                        return lambda t: t.matmul(bankB[:, 0:128], ones[0:1, 0:128],
                                                  rowd[0:1, par * 128:(par + 1) * 128],
                                                  start=True, stop=True)
                    p.op("tensor", mkgb(), reads=["rowd", "ones"], writes=[bn])
                    def mkd0(h=h, bankB=bankB):
                        return lambda v: v.scalar_tensor_tensor(
                            out=dmt_all[:, h * 128:(h + 1) * 128], in0=bankB[:, 0:128],
                            scalar=gamT[:, h:h + 1], in1=maskc[:], op0=ALU.subtract, op1=ALU.add)
                    p.op("vector", mkd0(), reads=[bn, "gamT", "maskc"], writes=["dmt_all"])
                    def mklc(h=h, bankB=bankB):
                        return lambda v: v.tensor_copy(lamb_all[:, h * 128:(h + 1) * 128], bankB[:, 0:128])
                    p.op("vector", mklc(), reads=[bn], writes=["lamb_all"])
                p.op("scalar", lambda s: s.activation(dmt_all[:], dmt_all[:], AF.Exp),
                     reads=["dmt_all"], writes=["dmt_all"])
                p.op("scalar", lambda s: s.activation(lamb_all[:], lamb_all[:], AF.Exp),
                     reads=["lamb_all"], writes=["lamb_all"])

                def head_ctx(h):
                    par = h % 2
                    return dict(
                        par=par, b=64 * par, kv=h // 4, qt=h // 2, h0=h * 128,
                        bankW=((P1, "P1"), (P4, "P4"))[par],
                        bankS=((P2, "P2"), (P5, "P5"))[par],
                        dmn=f"dms{par}", dm_ap=dms[:, par * 128:(par + 1) * 128])

                for hp in range(8):
                    ctxs = [head_ctx(2 * hp), head_ctx(2 * hp + 1)]
                    for C in ctxs:
                        b, qt = C["b"], C["qt"]
                        q_ap = qkvs[b:b + 64, qt * 128:(qt + 1) * 128]
                        def mkqs(q_ap=q_ap, b=b):
                            return lambda v: v.tensor_scalar(out=qs_b[b:b + 64, :], in0=q_ap, scalar1=0.125,
                                                             scalar2=None, op0=ALU.mult)
                        p.op("vector", mkqs(), reads=["qkvs"], writes=[f"qs{b}"])
                    for app in range(3):
                        zsel = ((qs_b, "qs"), (x1_b, "x1"), (x2_b, "x2"))[app]
                        Ssel = (("skk",), ("skk",), ("skv",))[app][0]
                        vsel = ((ktm, "ktm"), (ktm, "ktm"), (vtm, "vtm"))[app]
                        for C in ctxs:
                            b, kv, h = C["b"], C["kv"], C["qt"] * 2 + C["par"]
                            bankW, wn_ = C["bankW"]
                            bankS, sn_ = C["bankS"]
                            zbuf, znb = zsel
                            zn = f"{znb}{b}"
                            z_ap = zbuf[b:b + 64, 0:128]
                            Sbuf = skk if Ssel == "skk" else skv
                            S_ap = Sbuf[b:b + 64, (h // 2) * 64:(h // 2 + 1) * 64]
                            def mkzs(S_ap=S_ap, z_ap=z_ap, b=b, bankW=bankW):
                                tp = (b, b) if b else None
                                return lambda t: t.matmul(bankW[b:b + 64, 0:128], S_ap, z_ap,
                                                          start=True, stop=True, tile_position=tp)
                            p.op("tensor", mkzs(), reads=[Ssel, zn], writes=[wn_])
                            def mksc(kv=kv, b=b, z_ap=z_ap, bankS=bankS):
                                tp = (b, 0) if b else None
                                return lambda t: t.matmul(bankS[:, 0:128], kfm[kv][b:b + 64, 0:128], z_ap,
                                                          start=True, stop=True, tile_position=tp)
                            p.op("tensor", mksc(), reads=[f"kfm{kv}", zn], writes=[sn_])
                            def mkdm(h0=C["h0"], bankS=bankS, dm_ap=C["dm_ap"]):
                                return lambda v: v.tensor_tensor(out=dm_ap, in0=bankS[:, 0:128],
                                                                 in1=dmt_all[:, h0:h0 + 128], op=ALU.mult)
                            p.op("vector", mkdm(), reads=[sn_, "dmt_all"], writes=[C["dmn"]])
                        for C in ctxs:
                            b, kv = C["b"], C["kv"]
                            bankW, wn_ = C["bankW"]
                            vside, vsn = vsel
                            def mkin(vside=vside, kv=kv, b=b, bankW=bankW, dm_ap=C["dm_ap"]):
                                tp = (0, b) if b else None
                                return lambda t: t.matmul(bankW[b:b + 64, 128:256],
                                                          vside[:, kv * 64:(kv + 1) * 64],
                                                          dm_ap, start=True, stop=True, tile_position=tp)
                            p.op("tensor", mkin(), reads=[vsn, C["dmn"]], writes=[wn_])
                        for C in ctxs:
                            b, h0, qt = C["b"], C["h0"], C["qt"]
                            bankW, wn_ = C["bankW"]
                            def mktt2(b=b, bankW=bankW, h0=h0):
                                return lambda v: v.tensor_tensor(out=tt_b[b:b + 64, :],
                                                                 in0=bankW[b:b + 64, 0:128],
                                                                 in1=lamb_all[b:b + 64, h0:h0 + 128],
                                                                 op=ALU.mult)
                            p.op("vector", mktt2(), reads=[wn_, "lamb_all"], writes=[f"tt{b}"])
                            if app < 2:
                                nxt, nxb = ((x1_b, "x1"), (x2_b, "x2"))[app]
                                def mkuu(b=b, bankW=bankW):
                                    return lambda v: v.tensor_tensor(out=uu_b[b:b + 64, :],
                                                                     in0=tt_b[b:b + 64, :],
                                                                     in1=bankW[b:b + 64, 128:256], op=ALU.add)
                                p.op("vector", mkuu(), reads=[f"tt{b}", wn_], writes=[f"uu{b}"])
                                def mkx(nxt=nxt, b=b):
                                    return lambda v: v.scalar_tensor_tensor(out=nxt[b:b + 64, :],
                                                                            in0=uu_b[b:b + 64, :], scalar=-0.25,
                                                                            in1=qs_b[b:b + 64, :],
                                                                            op0=ALU.mult, op1=ALU.add)
                                p.op("vector", mkx(), reads=[f"uu{b}", f"qs{b}"], writes=[f"{nxb}{b}"])
                            else:
                                def mkuo(b=b, bankW=bankW, qt=qt):
                                    return lambda v: v.tensor_tensor(
                                        out=ocol[b:b + 64, qt * 128:(qt + 1) * 128],
                                        in0=tt_b[b:b + 64, :],
                                        in1=bankW[b:b + 64, 128:256], op=ALU.add)
                                p.op("vector", mkuo(), reads=[f"tt{b}", wn_], writes=["ocol"])
                    for C in ctxs:
                        par, kv, h0 = C["par"], C["kv"], C["h0"]
                        def mkkw(kv=kv, par=par, h0=h0):
                            return lambda v: v.tensor_scalar(out=kwv[:, par * 64:(par + 1) * 64],
                                                             in0=ktm[:, kv * 64:(kv + 1) * 64],
                                                             scalar1=dmt_all[:, h0 + 127:h0 + 128],
                                                             scalar2=None, op0=ALU.mult)
                        p.op("vector", mkkw(), reads=["ktm", "dmt_all"], writes=[f"kw{par}"])
                    for (Sn, msn) in (("skk", "ktm"), ("skv", "vtm")):
                        msrc = ktm if msn == "ktm" else vtm
                        Sbuf = skk if Sn == "skk" else skv
                        for C in ctxs:
                            b, kv, par = C["b"], C["kv"], C["par"]
                            bankW, wn_ = C["bankW"]
                            def mkds(msrc=msrc, kv=kv, b=b, par=par, bankW=bankW):
                                tp = (0, b) if b else None
                                return lambda t: t.matmul(bankW[b:b + 64, 256:320],
                                                          kwv[:, par * 64:(par + 1) * 64],
                                                          msrc[:, kv * 64:(kv + 1) * 64],
                                                          start=True, stop=True, tile_position=tp)
                            p.op("tensor", mkds(), reads=[f"kw{par}", msn], writes=[wn_])
                        for C in ctxs:
                            b, par, h0, qt = C["b"], C["par"], C["h0"], C["qt"]
                            bankW, wn_ = C["bankW"]
                            h = qt * 2 + par
                            S_ap = Sbuf[b:b + 64, (h // 2) * 64:(h // 2 + 1) * 64]
                            def mks1(S_ap=S_ap, b=b, par=par, h0=h0):
                                return lambda v: v.tensor_scalar(out=s_tmp[b:b + 64, par * 64:(par + 1) * 64],
                                                                 in0=S_ap,
                                                                 scalar1=lamb_all[b:b + 64, h0 + 127:h0 + 128],
                                                                 scalar2=None, op0=ALU.mult)
                            p.op("vector", mks1(), reads=[Sn, "lamb_all"], writes=[f"st{par}"])
                            def mks2(S_ap=S_ap, b=b, par=par, bankW=bankW):
                                return lambda v: v.tensor_tensor(out=S_ap,
                                                                 in0=s_tmp[b:b + 64, par * 64:(par + 1) * 64],
                                                                 in1=bankW[b:b + 64, 256:320], op=ALU.add)
                            p.op("vector", mks2(), reads=[f"st{par}", wn_], writes=[Sn])

                p.op("scalar", lambda s: s.activation(sqs[:, 0:1024], ocol[:], AF.Square),
                     reads=["ocol"], writes=["sqs"])
                for slot in range(8):
                    bankW, wn_ = ((P1, "P1"), (P4, "P4"))[slot % 2]
                    def mkmm(slot=slot, bankW=bankW):
                        return lambda t: t.matmul(bankW[0:2, 320:448], ones2[:, 0:2],
                                                  sqs[:, slot * 128:(slot + 1) * 128],
                                                  start=True, stop=True)
                    p.op("tensor", mkmm(), reads=["sqs", "ones2"], writes=[wn_])
                    def mkme(slot=slot, bankW=bankW):
                        return lambda v: v.tensor_copy(ms2[0:2, slot * 128:(slot + 1) * 128],
                                                       bankW[0:2, 320:448])
                    p.op("vector", mkme(), reads=[wn_], writes=["ms2"])
                    bankS, sn_ = ((P2, "P2"), (P5, "P5"))[slot % 2]
                    def mkax(slot=slot, bankS=bankS):
                        return lambda t: t.matmul(bankS[0:2, 128:256], apreT[:, 2 * slot: 2 * slot + 2],
                                                  ident[:], start=True, stop=True)
                    p.op("tensor", mkax(), reads=["apreT", "ident"], writes=[sn_])
                    def mkaxe(slot=slot, bankS=bankS):
                        return lambda v: v.tensor_copy(ap2[0:2, slot * 128:(slot + 1) * 128],
                                                       bankS[0:2, 128:256])
                    p.op("vector", mkaxe(), reads=[sn_], writes=["ap2"])
                p.op("scalar", lambda s: s.activation(ap2[:], ap2[:], AF.Sigmoid),
                     reads=["ap2"], writes=["ap2"])
                p.op("scalar", lambda s: s.activation(a2q[:], ap2[:], AF.Square),
                     reads=["ap2"], writes=["a2q"])
                p.op("vector", lambda v: v.tensor_tensor(out=ms2[:], in0=ms2[:], in1=a2q[:],
                                                         op=ALU.mult),
                     reads=["ms2", "a2q"], writes=["ms2"])
                p.op("scalar", lambda s: s.activation(ms2[:], ms2[:], AF.Sqrt,
                                                      scale=1.0 / 64, bias=EPS),
                     reads=["ms2"], writes=["ms2"])
                p.op("vector", lambda v: v.reciprocal(ms2[:], ms2[:]),
                     reads=["ms2"], writes=["ms2"])
                p.op("vector", lambda v: v.tensor_tensor(out=a2q[:], in0=ms2[:], in1=ap2[:],
                                                         op=ALU.mult),
                     reads=["ms2", "ap2"], writes=["a2q"])
                for slot in range(8):
                    bankS, sn_ = ((P2, "P2"), (P5, "P5"))[slot % 2]
                    def mkut(slot=slot, bankS=bankS):
                        return lambda t: t.transpose(bankS[:, 256:258],
                                                     a2q[0:2, slot * 128:(slot + 1) * 128],
                                                     ident[0:2, 0:2])
                    p.op("tensor", mkut(), reads=["a2q", "ident"], writes=[sn_])
                    def mkute(slot=slot, bankS=bankS):
                        return lambda v: v.tensor_copy(uT[:, 2 * slot: 2 * slot + 2], bankS[:, 256:258])
                    p.op("vector", mkute(), reads=[sn_], writes=["uT"])
                for h in range(16):
                    par = h % 2
                    b = 64 * par
                    bankB, bn = ((P0, "P0"), (P3, "P3"))[par]
                    def mkux(h=h, bankB=bankB):
                        return lambda t: t.matmul(bankB[0:1, 256:384], uT[:, h:h + 1], ident[:],
                                                  start=True, stop=True)
                    p.op("tensor", mkux(), reads=["uT", "ident"], writes=[bn])
                    def mkuc(par=par, bankB=bankB):
                        return lambda s: s.activation(rowd[0:1, par * 128:(par + 1) * 128],
                                                      bankB[0:1, 256:384], AF.Identity)
                    p.op("scalar", mkuc(), reads=[bn], writes=["rowd"])
                    def mkub(b=b, par=par, bankB=bankB):
                        tp = (0, b) if b else None
                        return lambda t: t.matmul(bankB[b:b + 64, 384:512], ones[0:1, 0:64],
                                                  rowd[0:1, par * 128:(par + 1) * 128],
                                                  start=True, stop=True, tile_position=tp)
                    p.op("tensor", mkub(), reads=["rowd", "ones"], writes=[bn])
                    def mkof(h=h, b=b, bankB=bankB):
                        return lambda v: v.scalar_tensor_tensor(
                            out=ocol[b:b + 64, (h // 2) * 128:(h // 2) * 128 + 128],
                            in0=ocol[b:b + 64, (h // 2) * 128:(h // 2) * 128 + 128],
                            scalar=nw2[b:b + 64, 0:1],
                            in1=bankB[b:b + 64, 384:512], op0=ALU.mult, op1=ALU.mult)
                    p.op("vector", mkof(), reads=["ocol", "nw2", bn], writes=["ocol"])

                for nh in range(2):
                    for dc in range(8):
                        def mkwo(nh=nh, dc=dc):
                            ps = (P0, P1)[nh]
                            return lambda t: t.matmul(ps[:, 0:512],
                                                      ocol[:, dc * 128:(dc + 1) * 128],
                                                      WoS[:, dc * 1024 + nh * 512: dc * 1024 + nh * 512 + 512],
                                                      start=(dc == 0), stop=(dc == 7))
                        p.op("tensor", mkwo(), reads=["ocol", "WoS"], writes=[("P0", "P1")[nh]])
                p.op("scalar", lambda s: s.activation(aout[:, 0:512], P0[:, 0:512], AF.Identity,
                                                      accum_out=cols[:, 8:9]),
                     reads=["P0"], writes=["aout", "colsA"])
                p.op("scalar", lambda s: s.activation(aout[:, 512:1024], P1[:, 0:512], AF.Identity,
                                                      accum_out=cols[:, 9:10]),
                     reads=["P1"], writes=["aout", "colsA"])
                p.op("vector", lambda v: v.tensor_tensor(out=cols[:, 8:9], in0=cols[:, 8:9],
                                                         in1=cols[:, 9:10], op=ALU.add),
                     reads=["colsA"], writes=["colsA"])
                p.op("scalar", lambda s: s.activation(sqs[:, 0:1024], aout[:, 0:1024], AF.Square,
                                                      accum_out=cols[:, 10:11]),
                     reads=["aout"], writes=["sqs", "colsA"])
                p.op("vector", lambda v: v.tensor_scalar(out=cols[:, 8:9], in0=cols[:, 8:9],
                                                         scalar1=1.0 / D, scalar2=None, op0=ALU.mult),
                     reads=["colsA"], writes=["colsA"])
                p.op("scalar", lambda s: s.activation(cols[:, 11:12], cols[:, 8:9], AF.Square),
                     reads=["colsA"], writes=["colsA"])
                p.op("vector", lambda v: v.scalar_tensor_tensor(out=cols[:, 12:13], in0=cols[:, 10:11],
                                                                scalar=1.0 / D, in1=cols[:, 11:12],
                                                                op0=ALU.mult, op1=ALU.subtract),
                     reads=["colsA"], writes=["colsA"])
                p.op("scalar", lambda s: s.activation(cols[:, 13:14], cols[:, 12:13], AF.Sqrt, bias=LN_EPS),
                     reads=["colsA"], writes=["colsA"])
                p.op("vector", lambda v: v.reciprocal(cols[:, 13:14], cols[:, 13:14]),
                     reads=["colsA"], writes=["colsA"])
                p.op("vector", lambda v: v.tensor_scalar(out=aout[:, 0:1024], in0=aout[:, 0:1024],
                                                         scalar1=cols[:, 8:9], scalar2=cols[:, 13:14],
                                                         op0=ALU.subtract, op1=ALU.mult),
                     reads=["aout", "colsA"], writes=["aout"])
                def mkres(xbuf=xbuf, xnm=xnm):
                    return lambda v: v.tensor_tensor(out=aout[:, 0:1024], in0=aout[:, 0:1024],
                                                     in1=xbuf[:], op=ALU.add)
                p.op("vector", mkres(), reads=["aout", xnm], writes=["aout"])
                p.dma("sync", xprime[c * 128:(c + 1) * 128, :], aout[:, 0:1024],
                      reads=["aout"], writes=["xprime_hbm"])

            p.emit(blkA, sems, start=0)
        mB = p.new_phase()
        with contextlib.ExitStack() as sB:
            EB = sB.enter_context
            xpt = EB(nc.sbuf_tensor("xpt", [128, 4 * 1024], F32))
            xntm = EB(nc.sbuf_tensor("xntm", [128, 1024], F32))
            xnfm = EB(nc.sbuf_tensor("xnfm", [128, 8 * 512], F32))
            w1s = [EB(nc.sbuf_tensor(f"w1s{i}", [128, 8 * FS], F32)) for i in range(2)]
            w3s = [EB(nc.sbuf_tensor(f"w3s{i}", [128, 8 * FS], F32)) for i in range(2)]
            wds = [EB(nc.sbuf_tensor(f"wds{i}", [128, (FS // 128) * 1024], F32)) for i in range(2)]
            b1s = [EB(nc.sbuf_tensor(f"b1s{i}", [128, 2 * FS], F32)) for i in range(2)]
            b3s = [EB(nc.sbuf_tensor(f"b3s{i}", [128, 2 * FS], F32)) for i in range(2)]
            a2s = [EB(nc.sbuf_tensor(f"a2s{i}", [128, (FS // 128) * 256], F32)) for i in range(2)]
            c1sb = EB(nc.sbuf_tensor("c1sb", [128, 2 * 512], F32))
            c3sb = EB(nc.sbuf_tensor("c3sb", [128, 2 * 512], F32))
            t1sb = EB(nc.sbuf_tensor("t1sb", [128, 3 * 512], F32))
            s1sb = EB(nc.sbuf_tensor("s1sb", [128, 3 * 512], F32))
            t3sb = EB(nc.sbuf_tensor("t3sb", [128, 3 * 512], F32))
            accf = EB(nc.sbuf_tensor("accf", [128, 2 * 512], F32))
            h1sb = EB(nc.sbuf_tensor("h1sb", [128, 2 * 512], F32))
            h3sb = EB(nc.sbuf_tensor("h3sb", [128, 2 * 512], F32))
            w8big = EB(nc.sbuf_tensor("w8big", [128, 8 * 512], F32))
            zsb = EB(nc.sbuf_tensor("zsb", [128, 2 * 512], F32))
            acctm = EB(nc.sbuf_tensor("acctm", [128, 4 * 1024], F32))
            lgt = EB(nc.sbuf_tensor("lgt", [128, 4 * 8], F32))
            rt = EB(nc.sbuf_tensor("rt", [128, 64], F32))
            w8tm = EB(nc.sbuf_tensor("w8tm", [128, 4 * 8], F32))
            outsb = EB(nc.sbuf_tensor("outsb", [128, 1024], F32))
            colsB = EB(nc.sbuf_tensor("colsB", [128, 16], F32))
            PD0 = EB(nc.psum_tensor("PD0", [128, 512], F32))
            PD1 = EB(nc.psum_tensor("PD1", [128, 512], F32))
            PD2 = EB(nc.psum_tensor("PD2", [128, 512], F32))
            PD3 = EB(nc.psum_tensor("PD3", [128, 512], F32))
            PW = EB(nc.psum_tensor("PW", [128, 1024], F32))
            PZ = EB(nc.psum_tensor("PZ", [128, 512], F32))
            blkB = EB(nc.Block())

            def load_stripe(s, bi):
                sl = slice(s * FS, (s + 1) * FS)
                p.dma("sync", w1s[bi][:].rearrange("p (c f) -> p c f", c=8),
                      w1T[0:D, sl].rearrange("(c p) f -> p c f", p=128),
                      reads=[f"w1s{bi}"], writes=[f"w1s{bi}"])
                p.dma("sync", w3s[bi][:].rearrange("p (c f) -> p c f", c=8),
                      w3T[0:D, sl].rearrange("(c p) f -> p c f", p=128),
                      reads=[f"w3s{bi}"], writes=[f"w3s{bi}"])
                p.dma("sync", wds[bi][:].rearrange("p (c d) -> p c d", c=FS // 128),
                      wdT[sl, :].rearrange("(c p) d -> p c d", p=128),
                      reads=[f"wds{bi}"], writes=[f"wds{bi}"])
                p.dma("sync", b1s[bi][:, 0:FS], B1Tg[0:128, sl], reads=[f"b1s{bi}"], writes=[f"b1s{bi}"])
                p.dma("sync", b1s[bi][:, FS:2 * FS], B1Tg[128:256, sl], reads=[f"b1s{bi}"], writes=[f"b1s{bi}"])
                p.dma("sync", b3s[bi][:, 0:FS], B3Tg[0:128, sl], reads=[f"b3s{bi}"], writes=[f"b3s{bi}"])
                p.dma("sync", b3s[bi][:, FS:2 * FS], B3Tg[128:256, sl], reads=[f"b3s{bi}"], writes=[f"b3s{bi}"])
                p.dma("sync", a2s[bi][:].rearrange("p (c d) -> p c d", c=FS // 128),
                      A2T[sl, :].rearrange("(c p) d -> p c d", p=128),
                      reads=[f"a2s{bi}"], writes=[f"a2s{bi}"])

            for tt in range(NTT):
                t0 = tt * TT_TOK
                for sub in range(NSUB):
                    p.dma("sync", xpt[:, sub * 1024:(sub + 1) * 1024],
                          xprime[t0 + sub * 128: t0 + (sub + 1) * 128, :],
                          reads=["xpt", "xprime_hbm"], writes=["xpt"])
                for sub in range(NSUB):
                    src = xpt[:, sub * 1024:(sub + 1) * 1024]
                    dst = xntm[:, 0:1024]
                    p.op("scalar", lambda s, src=src: s.activation(t3sb[:, 0:1024], src, AF.Identity,
                                                                   accum_out=colsB[:, 0:1]),
                         reads=["xpt"], writes=["t3sb", "colsB"])
                    p.op("scalar", lambda s, src=src: s.activation(t3sb[:, 0:1024], src, AF.Square,
                                                                   accum_out=colsB[:, 1:2]),
                         reads=["xpt"], writes=["t3sb", "colsB"])
                    p.op("vector", lambda v: v.tensor_scalar(out=colsB[:, 0:1], in0=colsB[:, 0:1],
                                                             scalar1=1.0 / D, scalar2=None, op0=ALU.mult),
                         reads=["colsB"], writes=["colsB"])
                    p.op("scalar", lambda s: s.activation(colsB[:, 2:3], colsB[:, 0:1], AF.Square),
                         reads=["colsB"], writes=["colsB"])
                    p.op("vector", lambda v: v.scalar_tensor_tensor(out=colsB[:, 3:4], in0=colsB[:, 1:2],
                                                                    scalar=1.0 / D, in1=colsB[:, 2:3],
                                                                    op0=ALU.mult, op1=ALU.subtract),
                         reads=["colsB"], writes=["colsB"])
                    p.op("scalar", lambda s: s.activation(colsB[:, 4:5], colsB[:, 3:4], AF.Sqrt, bias=LN_EPS),
                         reads=["colsB"], writes=["colsB"])
                    p.op("vector", lambda v: v.reciprocal(colsB[:, 4:5], colsB[:, 4:5]),
                         reads=["colsB"], writes=["colsB"])
                    p.op("vector", lambda v, src=src, dst=dst: v.tensor_scalar(
                        out=dst, in0=src, scalar1=colsB[:, 0:1], scalar2=colsB[:, 4:5],
                        op0=ALU.subtract, op1=ALU.mult),
                         reads=["xpt", "colsB"], writes=["xntm"])
                    for dc in range(8):
                        def mkt(sub=sub, dc=dc):
                            return lambda t: t.transpose(
                                PD0[:, 0:128], xntm[:, dc * 128:(dc + 1) * 128], ident[:])
                        p.op("tensor", mkt(), reads=["xntm", "ident"], writes=["PD0"])
                        def mke(sub=sub, dc=dc):
                            return lambda v: v.tensor_copy(
                                xnfm[:, dc * TT_TOK + sub * 128: dc * TT_TOK + (sub + 1) * 128], PD0[:, 0:128])
                        p.op("vector", mke(), reads=["PD0"], writes=["xnfm"])

                for dc in range(8):
                    def mkg(dc=dc):
                        return lambda t: t.matmul(PD1[0:8, 0:TT_TOK], gwT[:, dc * 8:(dc + 1) * 8],
                                                  xnfm[:, dc * TT_TOK:(dc + 1) * TT_TOK],
                                                  start=(dc == 0), stop=(dc == 7))
                    p.op("tensor", mkg(), reads=["gwT", "xnfm"], writes=["PD1"])
                p.op("vector", lambda v: v.tensor_copy(zsb[0:8, 0:TT_TOK], PD1[0:8, 0:TT_TOK]),
                     reads=["PD1"], writes=["zsb"])
                for sub in range(NSUB):
                    def mktl(sub=sub):
                        return lambda t: t.transpose(PD2[:, 0:8], zsb[0:8, sub * 128:(sub + 1) * 128],
                                                     ident[0:8, 0:8])
                    p.op("tensor", mktl(), reads=["zsb", "ident"], writes=["PD2"])
                    def mkel(sub=sub):
                        return lambda v: v.tensor_copy(lgt[:, sub * 8:(sub + 1) * 8], PD2[:, 0:8])
                    p.op("vector", mkel(), reads=["PD2"], writes=["lgt"])

                for sub in range(NSUB):
                    lg_ap = lgt[:, sub * 8:(sub + 1) * 8]
                    p.op("vector", lambda v, lg_ap=lg_ap: v.tensor_reduce(out=rt[:, 0:1], in_=lg_ap,
                                                                          axis=AX.X, op=ALU.max),
                         reads=["lgt"], writes=["rt"])
                    p.op("vector", lambda v, lg_ap=lg_ap: v.tensor_scalar(out=rt[:, 16:24], in0=lg_ap,
                                                                          scalar1=rt[:, 0:1], scalar2=None,
                                                                          op0=ALU.is_equal),
                         reads=["lgt", "rt"], writes=["rt"])
                    p.op("vector", lambda v: v.scalar_tensor_tensor(out=rt[:, 16:24], in0=rt[:, 16:24],
                                                                    scalar=-1024.0, in1=iota8[:],
                                                                    op0=ALU.mult, op1=ALU.add),
                         reads=["rt", "iota8"], writes=["rt"])
                    p.op("vector", lambda v: v.tensor_reduce(out=rt[:, 1:2], in_=rt[:, 16:24],
                                                             axis=AX.X, op=ALU.min),
                         reads=["rt"], writes=["rt"])
                    p.op("vector", lambda v: v.tensor_scalar(out=rt[:, 16:24], in0=rt[:, 16:24],
                                                             scalar1=rt[:, 1:2], scalar2=None,
                                                             op0=ALU.is_equal),
                         reads=["rt"], writes=["rt"])
                    p.op("vector", lambda v, lg_ap=lg_ap: v.scalar_tensor_tensor(
                        out=rt[:, 24:32], in0=rt[:, 16:24], scalar=NEG, in1=lg_ap,
                        op0=ALU.mult, op1=ALU.add),
                         reads=["rt", "lgt"], writes=["rt"])
                    p.op("vector", lambda v: v.tensor_reduce(out=rt[:, 2:3], in_=rt[:, 24:32],
                                                             axis=AX.X, op=ALU.max),
                         reads=["rt"], writes=["rt"])
                    p.op("vector", lambda v: v.tensor_scalar(out=rt[:, 32:40], in0=rt[:, 24:32],
                                                             scalar1=rt[:, 2:3], scalar2=None,
                                                             op0=ALU.is_equal),
                         reads=["rt"], writes=["rt"])
                    p.op("vector", lambda v: v.scalar_tensor_tensor(out=rt[:, 32:40], in0=rt[:, 32:40],
                                                                    scalar=-1024.0, in1=iota8[:],
                                                                    op0=ALU.mult, op1=ALU.add),
                         reads=["rt", "iota8"], writes=["rt"])
                    p.op("vector", lambda v: v.tensor_reduce(out=rt[:, 3:4], in_=rt[:, 32:40],
                                                             axis=AX.X, op=ALU.min),
                         reads=["rt"], writes=["rt"])
                    p.op("vector", lambda v: v.tensor_scalar(out=rt[:, 32:40], in0=rt[:, 32:40],
                                                             scalar1=rt[:, 3:4], scalar2=None,
                                                             op0=ALU.is_equal),
                         reads=["rt"], writes=["rt"])
                    p.op("vector", lambda v: v.tensor_scalar(out=rt[:, 4:5], in0=rt[:, 0:1], scalar1=-1.0,
                                                             scalar2=None, op0=ALU.mult),
                         reads=["rt"], writes=["rt"])
                    p.op("scalar", lambda s: s.activation(rt[:, 5:6], rt[:, 2:3], AF.Sigmoid, bias=rt[:, 4:5]),
                         reads=["rt"], writes=["rt"])
                    p.op("vector", lambda v: v.tensor_scalar(out=rt[:, 6:7], in0=rt[:, 5:6], scalar1=-1.0,
                                                             scalar2=1.0, op0=ALU.mult, op1=ALU.add),
                         reads=["rt"], writes=["rt"])
                    p.op("vector", lambda v: v.tensor_scalar(out=rt[:, 48:56], in0=rt[:, 16:24],
                                                             scalar1=rt[:, 6:7], scalar2=None, op0=ALU.mult),
                         reads=["rt"], writes=["rt"])
                    def mkw8(sub=sub):
                        return lambda v: v.scalar_tensor_tensor(out=w8tm[:, sub * 8:(sub + 1) * 8],
                                                                in0=rt[:, 32:40], scalar=rt[:, 5:6],
                                                                in1=rt[:, 48:56], op0=ALU.mult, op1=ALU.add)
                    p.op("vector", mkw8(), reads=["rt"], writes=["w8tm"])

                for e in range(E):
                    for sub in range(NSUB):
                        def mkex(e=e, sub=sub):
                            return lambda t: t.matmul(PD3[0:1, sub * 128:(sub + 1) * 128],
                                                      w8tm[:, sub * 8 + e: sub * 8 + e + 1], ident[:],
                                                      start=True, stop=True)
                        p.op("tensor", mkex(), reads=["w8tm", "ident"], writes=["PD3"])
                    p.op("scalar", lambda s: s.activation(outsb[0:1, 0:TT_TOK], PD3[0:1, 0:TT_TOK], AF.Identity),
                         reads=["PD3"], writes=["outsb"])
                    p.op("tensor", lambda t: t.matmul(PD2[:, 0:TT_TOK], ones[0:1, 0:128], outsb[0:1, 0:TT_TOK],
                                                      start=True, stop=True),
                         reads=["outsb", "ones"], writes=["PD2"])
                    def mkew(e=e):
                        return lambda v: v.tensor_copy(w8big[:, e * TT_TOK:(e + 1) * TT_TOK], PD2[:, 0:TT_TOK])
                    p.op("vector", mkew(), reads=["PD2"], writes=["w8big"])

                for (Asrc, hdst, an, hn) in ((A1Tg, h1sb, "ATa1", "h1sb"), (A3Tg, h3sb, "ATa3", "h3sb")):
                    for g in range(2):
                        for dc in range(8):
                            def mkh(Asrc=Asrc, g=g, dc=dc):
                                return lambda t: t.matmul(PD0[:, 0:TT_TOK], Asrc[g][:, dc * 128:(dc + 1) * 128],
                                                          xnfm[:, dc * TT_TOK:(dc + 1) * TT_TOK],
                                                          start=(dc == 0), stop=(dc == 7))
                            p.op("tensor", mkh(), reads=[f"{an}{g}", "xnfm"], writes=["PD0"])
                        def mkhe(hdst=hdst, g=g, hn=hn):
                            return lambda v: v.tensor_copy(hdst[:, g * TT_TOK:(g + 1) * TT_TOK], PD0[:, 0:TT_TOK])
                        p.op("vector", mkhe(), reads=["PD0"], writes=[hn])

                p.op("gpsimd", lambda g: g.memset(acctm[:, 0:NSUB * 1024], 0.0),
                     reads=["acctm"], writes=["acctm"])
                p.op("gpsimd", lambda g: g.memset(zsb[:], 0.0),
                     reads=["zsb"], writes=["zsb"])

                if tt == 0:
                    load_stripe(0, 0)
                for s in range(NS):
                    bi = s % 2
                    if s + 1 < NS:
                        load_stripe(s + 1, 1 - bi)
                    elif tt + 1 < NTT:
                        load_stripe(0, 1 - bi)
                    for (ws, wn, psl, psn) in ((w1s, "w1s", (PD0, PD1), ("PD0", "PD1")),
                                               (w3s, "w3s", (PD2, PD3), ("PD2", "PD3"))):
                        for m in range(2):
                            for dc in range(8):
                                def mkcm(ws=ws, bi=bi, m=m, dc=dc, ps=psl[m]):
                                    return lambda t: t.matmul(
                                        ps[:, 0:TT_TOK],
                                        ws[bi][:, dc * FS + m * 128: dc * FS + m * 128 + 128],
                                        xnfm[:, dc * TT_TOK:(dc + 1) * TT_TOK],
                                        start=(dc == 0), stop=(dc == 7))
                                p.op("tensor", mkcm(), reads=[f"{wn}{bi}", "xnfm"], writes=[psn[m]])
                    for m in range(2):
                        def mkec(m=m):
                            return lambda v: v.tensor_copy(c1sb[:, m * 512: m * 512 + TT_TOK],
                                                           (PD0, PD1)[m][:, 0:TT_TOK])
                        p.op("vector", mkec(), reads=[("PD0", "PD1")[m]], writes=["c1sb"])
                        def mkec3(m=m):
                            return lambda v: v.tensor_copy(c3sb[:, m * 512: m * 512 + TT_TOK],
                                                           (PD2, PD3)[m][:, 0:TT_TOK])
                        p.op("vector", mkec3(), reads=[("PD2", "PD3")[m]], writes=["c3sb"])

                    pending_z = []

                    def flush_z():
                        for fnz, rd, eidx in pending_z:
                            if fnz is None:
                                eidx()
                            else:
                                p.op("tensor", fnz, reads=rd, writes=["PZ"])
                        pending_z.clear()

                    for e in range(E):
                        g, el = divmod(e, 4)
                        tp = (el * 32, 0) if el else None
                        for m in range(2):
                            step = 2 * e + m
                            q2 = step % 2
                            q3 = step % 3
                            pd1 = (PD0, PD1)[q2]
                            pd3 = (PD2, PD3)[q2]
                            pn1 = ("PD0", "PD1")[q2]
                            pn3 = ("PD2", "PD3")[q2]
                            th = q3 * 512
                            tn1, tn3, tns = f"t1sb{q3}", f"t3sb{q3}", f"s1sb{q3}"
                            def mkd1(g=g, el=el, bi=bi, m=m, tp=tp, pd1=pd1):
                                return lambda t: t.matmul(
                                    pd1[:, 0:TT_TOK],
                                    b1s[bi][el * 32: el * 32 + 16, g * FS + m * 128: g * FS + m * 128 + 128],
                                    h1sb[el * 32: el * 32 + 16, g * TT_TOK:(g + 1) * TT_TOK],
                                    start=True, stop=True, tile_position=tp)
                            p.op("tensor", mkd1(), reads=[f"b1s{bi}", "h1sb"], writes=[pn1])
                            def mkd3(g=g, el=el, bi=bi, m=m, tp=tp, pd3=pd3):
                                return lambda t: t.matmul(
                                    pd3[:, 0:TT_TOK],
                                    b3s[bi][el * 32: el * 32 + 16, g * FS + m * 128: g * FS + m * 128 + 128],
                                    h3sb[el * 32: el * 32 + 16, g * TT_TOK:(g + 1) * TT_TOK],
                                    start=True, stop=True, tile_position=tp)
                            p.op("tensor", mkd3(), reads=[f"b3s{bi}", "h3sb"], writes=[pn3])
                            flush_z()
                            def mkt1(m=m, pd1=pd1, th=th):
                                return lambda v: v.scalar_tensor_tensor(
                                    out=t1sb[:, th: th + TT_TOK],
                                    in0=pd1[:, 0:TT_TOK], scalar=SCALE,
                                    in1=c1sb[:, m * 512: m * 512 + TT_TOK], op0=ALU.mult, op1=ALU.add)
                            p.op("vector", mkt1(), reads=[pn1, "c1sb"], writes=[tn1])
                            def mks1a(th=th):
                                return lambda s: s.activation(s1sb[:, th: th + TT_TOK],
                                                              t1sb[:, th: th + TT_TOK], AF.Silu)
                            p.op("scalar", mks1a(), reads=[tn1], writes=[tns])
                            def mkt3(m=m, pd3=pd3, th=th):
                                return lambda v: v.scalar_tensor_tensor(
                                    out=t3sb[:, th: th + TT_TOK],
                                    in0=pd3[:, 0:TT_TOK], scalar=SCALE,
                                    in1=c3sb[:, m * 512: m * 512 + TT_TOK], op0=ALU.mult, op1=ALU.add)
                            p.op("vector", mkt3(), reads=[pn3, "c3sb"], writes=[tn3])
                            def mkt3w(th=th, e=e):
                                return lambda v: v.tensor_tensor(
                                    out=t3sb[:, th: th + TT_TOK],
                                    in0=t3sb[:, th: th + TT_TOK],
                                    in1=w8big[:, e * TT_TOK:(e + 1) * TT_TOK], op=ALU.mult)
                            p.op("vector", mkt3w(), reads=[tn3, "w8big"], writes=[tn3])
                            def mkp(th=th):
                                return lambda v: v.tensor_tensor(
                                    out=t1sb[:, th: th + TT_TOK],
                                    in0=s1sb[:, th: th + TT_TOK],
                                    in1=t3sb[:, th: th + TT_TOK], op=ALU.mult)
                            p.op("vector", mkp(), reads=[tns, tn3], writes=[tn1])
                            if e == 0:
                                def mkacc(m=m, th=th):
                                    return lambda v: v.tensor_copy(accf[:, m * 512: m * 512 + TT_TOK],
                                                                   t1sb[:, th: th + TT_TOK])
                                p.op("vector", mkacc(), reads=[tn1], writes=["accf"])
                            else:
                                def mkacc(m=m, th=th):
                                    return lambda v: v.tensor_tensor(
                                        out=accf[:, m * 512: m * 512 + TT_TOK],
                                        in0=accf[:, m * 512: m * 512 + TT_TOK],
                                        in1=t1sb[:, th: th + TT_TOK], op=ALU.add)
                                p.op("vector", mkacc(), reads=[tn1, "accf"], writes=["accf"])
                            def mkz(m=m, g=g, el=el, bi=bi, th=th):
                                tpz = (0, el * 32) if el else None
                                return lambda t: t.matmul(
                                    PZ[el * 32: el * 32 + 32, 0:TT_TOK],
                                    a2s[bi][:, m * 256 + g * 128 + el * 32: m * 256 + g * 128 + el * 32 + 32],
                                    t1sb[:, th: th + TT_TOK],
                                    start=(m == 0), stop=(m == 1),
                                    tile_position=tpz)
                            pending_z.append((mkz(), [f"a2s{bi}", tn1], e))
                            if m == 1:
                                pass
                        def mkza_pending(g=g, el=el, e=e):
                            def emit_za():
                                def mkza():
                                    return lambda v: v.tensor_tensor(
                                        out=zsb[el * 32: el * 32 + 32, g * 512: g * 512 + TT_TOK],
                                        in0=zsb[el * 32: el * 32 + 32, g * 512: g * 512 + TT_TOK],
                                        in1=PZ[el * 32: el * 32 + 32, 0:TT_TOK], op=ALU.add)
                                p.op("vector", mkza(), reads=["zsb", "PZ"], writes=["zsb"])
                            return emit_za
                        pending_za = mkza_pending()
                        if e == E - 1:
                            flush_z()
                            pending_za()
                        else:
                            pending_z.append((None, None, pending_za))
                    for msub in range(NSUB):
                        for ch in range(FS // 128):
                            for nh in range(2):
                                def mkdn(msub=msub, ch=ch, nh=nh, bi=bi):
                                    return lambda t: t.matmul(
                                        PW[:, nh * 512:(nh + 1) * 512],
                                        accf[:, ch * 512 + msub * 128: ch * 512 + msub * 128 + 128],
                                        wds[bi][:, ch * 1024 + nh * 512: ch * 1024 + (nh + 1) * 512],
                                        start=(ch == 0), stop=(ch == FS // 128 - 1))
                                p.op("tensor", mkdn(), reads=["accf", f"wds{bi}"], writes=["PW"])
                        def mkda(msub=msub):
                            return lambda v: v.tensor_tensor(out=acctm[:, msub * 1024:(msub + 1) * 1024],
                                                             in0=acctm[:, msub * 1024:(msub + 1) * 1024],
                                                             in1=PW[:, 0:1024], op=ALU.add)
                        p.op("vector", mkda(), reads=["acctm", "PW"], writes=["acctm"])

                for msub in range(NSUB):
                    for nh in range(2):
                        for g in range(2):
                            def mko2m(msub=msub, nh=nh, g=g):
                                return lambda t: t.matmul(
                                    PW[:, nh * 512:(nh + 1) * 512],
                                    zsb[:, g * 512 + msub * 128: g * 512 + msub * 128 + 128],
                                    B2Tg[g][:, nh * 512:(nh + 1) * 512],
                                    start=(g == 0), stop=(g == 1))
                            p.op("tensor", mko2m(), reads=["zsb", f"B2T{g}"], writes=["PW"])
                    def mko2a(msub=msub):
                        return lambda v: v.scalar_tensor_tensor(
                            out=acctm[:, msub * 1024:(msub + 1) * 1024],
                            in0=PW[:, 0:1024], scalar=SCALE,
                            in1=acctm[:, msub * 1024:(msub + 1) * 1024], op0=ALU.mult, op1=ALU.add)
                    p.op("vector", mko2a(), reads=["acctm", "PW"], writes=["acctm"])

                for sub in range(NSUB):
                    src = acctm[:, sub * 1024:(sub + 1) * 1024]
                    p.op("scalar", lambda s, src=src: s.activation(t3sb[:, 0:1024], src, AF.Identity,
                                                                   accum_out=colsB[:, 8:9]),
                         reads=["acctm"], writes=["t3sb", "colsB"])
                    p.op("scalar", lambda s, src=src: s.activation(t3sb[:, 0:1024], src, AF.Square,
                                                                   accum_out=colsB[:, 9:10]),
                         reads=["acctm"], writes=["t3sb", "colsB"])
                    p.op("vector", lambda v: v.tensor_scalar(out=colsB[:, 8:9], in0=colsB[:, 8:9],
                                                             scalar1=1.0 / D, scalar2=None, op0=ALU.mult),
                         reads=["colsB"], writes=["colsB"])
                    p.op("scalar", lambda s: s.activation(colsB[:, 10:11], colsB[:, 8:9], AF.Square),
                         reads=["colsB"], writes=["colsB"])
                    p.op("vector", lambda v: v.scalar_tensor_tensor(out=colsB[:, 11:12], in0=colsB[:, 9:10],
                                                                    scalar=1.0 / D, in1=colsB[:, 10:11],
                                                                    op0=ALU.mult, op1=ALU.subtract),
                         reads=["colsB"], writes=["colsB"])
                    p.op("scalar", lambda s: s.activation(colsB[:, 12:13], colsB[:, 11:12], AF.Sqrt,
                                                          bias=LN_EPS),
                         reads=["colsB"], writes=["colsB"])
                    p.op("vector", lambda v: v.reciprocal(colsB[:, 12:13], colsB[:, 12:13]),
                         reads=["colsB"], writes=["colsB"])
                    p.op("vector", lambda v, src=src: v.tensor_scalar(out=outsb[:], in0=src,
                                                                      scalar1=colsB[:, 8:9],
                                                                      scalar2=colsB[:, 12:13],
                                                                      op0=ALU.subtract, op1=ALU.mult),
                         reads=["acctm", "colsB"], writes=["outsb"])
                    def mkfin(sub=sub):
                        return lambda v: v.tensor_tensor(out=outsb[:], in0=outsb[:],
                                                         in1=xpt[:, sub * 1024:(sub + 1) * 1024], op=ALU.add)
                    p.op("vector", mkfin(), reads=["outsb", "xpt"], writes=["outsb"])
                    p.dma("sync", y_out[t0 + sub * 128: t0 + (sub + 1) * 128, :], outsb[:],
                          reads=["outsb"], writes=["y_hbm"])

            p.emit(blkB, sems, start=mB, closer=True)
    return nc


def kernel(**inputs):
    x = np.ascontiguousarray(np.asarray(inputs["L0_attn_x"], dtype=np.float32))
    B = x.shape[0]

    cvals = np.broadcast_to(np.array(CONST_VALS, np.float32), (128, 8)).copy()
    ident = np.eye(128, dtype=np.float32)
    idsh = (np.eye(128, k=64) + np.eye(128, k=-64)).astype(np.float32)
    mask = np.where(np.arange(128)[None, :] >= np.arange(128)[:, None], 0.0, NEG).astype(np.float32)
    iota8 = np.broadcast_to(np.arange(8, dtype=np.float32), (128, 8)).copy()
    ones = np.ones((128, 128), dtype=np.float32)
    nw2 = np.tile(np.asarray(inputs["L0_attn_norm_w"], np.float32), 2).reshape(128, 1)

    common = {
        "Wq": np.ascontiguousarray(inputs["L0_attn_Wq"], np.float32),
        "Wk": np.ascontiguousarray(inputs["L0_attn_Wk"], np.float32),
        "Wv": np.ascontiguousarray(inputs["L0_attn_Wv"], np.float32),
        "Wconv": np.ascontiguousarray(inputs["L0_attn_Wconv"], np.float32),
        "Wa": np.ascontiguousarray(inputs["L0_attn_Wa"], np.float32),
        "Walpha": np.ascontiguousarray(inputs["L0_attn_Walpha"], np.float32),
        "Wb": np.ascontiguousarray(inputs["L0_attn_Wb"], np.float32),
        "A_log": np.ascontiguousarray(inputs["L0_attn_A_log"], np.float32).reshape(16, 1),
        "dt_bias": np.ascontiguousarray(inputs["L0_attn_dt_bias"], np.float32).reshape(16, 1),
        "norm_w2": nw2,
        "Wo": np.ascontiguousarray(inputs["L0_attn_Wo"], np.float32),
        "gate_w": np.ascontiguousarray(inputs["L0_ffn_gate_w"], np.float32),
        "w_gate": np.ascontiguousarray(inputs["L0_ffn_w_gate"], np.float32),
        "w_up": np.ascontiguousarray(inputs["L0_ffn_w_up"], np.float32),
        "w_down": np.ascontiguousarray(inputs["L0_ffn_w_down"], np.float32),
        "A1": np.ascontiguousarray(inputs["L0_ffn_A1"], np.float32).reshape(E * R, D),
        "B1": np.ascontiguousarray(inputs["L0_ffn_B1"], np.float32).reshape(E * F, R),
        "A3": np.ascontiguousarray(inputs["L0_ffn_A3"], np.float32).reshape(E * R, D),
        "B3": np.ascontiguousarray(inputs["L0_ffn_B3"], np.float32).reshape(E * F, R),
        "A2": np.ascontiguousarray(inputs["L0_ffn_A2"], np.float32).reshape(E * R, F),
        "B2": np.ascontiguousarray(inputs["L0_ffn_B2"], np.float32).reshape(E * D, R),
        "cvals": cvals, "ident": ident, "idsh": idsh, "maskc": mask, "iota8": iota8, "ones": ones,
    }

    nc = build_nc(Tloc=x.shape[1])
    in_maps = [dict(common, x=np.ascontiguousarray(x[b])) for b in range(B)]
    res = run_bass_kernel_spmd(nc, in_maps, core_ids=list(range(B)))
    out = np.stack([res.results[b]["y"] for b in range(B)], axis=0)
    return out.astype(np.float32)


# version 09 (final)
# speedup vs baseline: 1.4450x; 1.1441x over previous
import contextlib
import numpy as np

import concourse.bass as bass
import concourse.mybir as mybir
from concourse.bass_utils import run_bass_kernel_spmd

F32 = mybir.dt.float32
AF = mybir.ActivationFunctionType
ALU = mybir.AluOpType
AX = mybir.AxisListType

D = 1024
CCH = 128
HQ, HKV, HK, HV = 16, 4, 64, 64
F = 4096
E = 8
R = 16
SCALE = 2.0
EPS = 1e-6
LN_EPS = 1e-5
NEG = -1e30
CONST_VALS = (1e-5, 1e-6, 1.0, -1.0, 1.0 / 64, 0.0, 2.0, 0.5)
FS = 256
NS = F // FS


class Prog:

    NSYNC = 4
    ENGINES = ("tensor", "vector", "scalar", "gpsimd",
               "sync0", "sync1", "sync2", "sync3")

    def __init__(self):
        self.ops = []
        self.count = {e: 0 for e in self.ENGINES}
        self.last_w = {}
        self.last_r = {}
        self.inc = {"tensor": 1, "vector": 1, "scalar": 1, "gpsimd": 1,
                    "sync0": 16, "sync1": 16, "sync2": 16, "sync3": 16}
        self.barrier_floor = {}
        self.barrier_done = set()
        self.sync_rr = 0

    def op(self, eng, fn, reads=(), writes=()):
        if eng == "sync":
            eng = f"sync{self.sync_rr % self.NSYNC}"
            self.sync_rr += 1
        deps = {}

        def dep(e2, c2):
            if c2 > 0:
                deps[e2] = max(deps.get(e2, 0), c2)

        if eng not in self.barrier_done:
            for e2, c2 in self.barrier_floor.items():
                dep(e2, c2)
            self.barrier_done.add(eng)
        for b in reads:
            w = self.last_w.get(b)
            if w:
                dep(*w)
        for b in writes:
            w = self.last_w.get(b)
            if w:
                dep(*w)
            for e2, c2 in self.last_r.get(b, {}).items():
                dep(e2, c2)
        self.count[eng] += 1
        c = self.count[eng]
        self.ops.append((eng, fn, deps))
        for b in reads:
            self.last_r.setdefault(b, {})[eng] = c
        for b in writes:
            self.last_w[b] = (eng, c)
            self.last_r[b] = {}

    def dma(self, queue, out, in_, reads=(), writes=()):
        self.op(queue, lambda q: q.dma_start(out=out, in_=in_), reads, writes)

    def new_phase(self):
        self.barrier_floor = dict(self.count)
        self.barrier_done = set()
        marker = len(self.ops)
        return marker

    def emit(self, block, sems, start=0, closer=False):
        per_eng = {e: [] for e in self.ENGINES}
        idx = {e: 0 for e in self.ENGINES}
        for i, (eng, fn, deps) in enumerate(self.ops):
            idx[eng] += 1
            if i >= start:
                per_eng[eng].append((fn, deps, idx[eng]))

        def mk(names):
            merged = []
            for nm in names:
                merged.extend((nm, fn, deps, c) for (fn, deps, c) in per_eng[nm])
            if len(names) > 1:
                order = {}
                for i, (e3, fn, deps) in enumerate(self.ops):
                    order.setdefault(e3, []).append(i)
                pos = {nm: 0 for nm in names}
                seq = []
                for i, (e3, fn, deps) in enumerate(self.ops):
                    if i < start or e3 not in names:
                        continue
                    seq.append((e3, fn, deps, None))
                merged = []
                cnt = dict.fromkeys(names, 0)
                k = {nm: 0 for nm in names}
                for i, (e3, fn, deps) in enumerate(self.ops):
                    if e3 in names:
                        k[e3] += 1
                        if i >= start:
                            merged.append((e3, fn, deps, k[e3]))

            def body(eng):
                waited = {}
                for nm, fn, deps, c in merged:
                    if nm.startswith("sync") and c > 1:
                        tgt = (c - 1) * self.inc[nm]
                        if waited.get(nm, 0) < tgt:
                            eng.wait_ge(sems[nm], tgt)
                            waited[nm] = tgt
                    for e2, c2 in deps.items():
                        tgt = c2 * self.inc[e2]
                        if waited.get(e2, 0) < tgt:
                            eng.wait_ge(sems[e2], tgt)
                            waited[e2] = tgt
                    instr = fn(eng)
                    instr.then_inc(sems[nm], self.inc[nm])
            return body

        sync_names = [n for n in self.ENGINES if n.startswith("sync")]
        for name in self.ENGINES:
            if name.startswith("sync"):
                continue
            if per_eng[name]:
                getattr(block, name)(mk([name]))
        if any(per_eng[n] for n in sync_names):
            block.sync(mk(sync_names))
        if closer:
            final = {e: self.count[e] * self.inc[e] for e in self.ENGINES}

            def cl(eng):
                for e2, tgt in final.items():
                    if tgt:
                        eng.wait_ge(sems[e2], tgt)
            block.sync(cl)


def build_nc(Tloc=2048):
    TT_TOK = min(512, Tloc)
    NTT = Tloc // TT_TOK
    NCH = Tloc // CCH
    NSUB = TT_TOK // 128

    nc = bass.Bass("TRN2", target_bir_lowering=False, debug=False)
    dt = nc.dram_tensor

    x_in = dt("x", [Tloc, D], F32, kind="ExternalInput").ap()
    y_out = dt("y", [Tloc, D], F32, kind="ExternalOutput").ap()

    Wq = dt("Wq", [D, 1024], F32, kind="ExternalInput").ap()
    Wk = dt("Wk", [D, 256], F32, kind="ExternalInput").ap()
    Wv = dt("Wv", [D, 256], F32, kind="ExternalInput").ap()
    Wconv = dt("Wconv", [1536, 4], F32, kind="ExternalInput").ap()
    Wa = dt("Wa", [D, 16], F32, kind="ExternalInput").ap()
    Walpha = dt("Walpha", [D, 16], F32, kind="ExternalInput").ap()
    Wb = dt("Wb", [D, 4], F32, kind="ExternalInput").ap()
    A_log = dt("A_log", [16, 1], F32, kind="ExternalInput").ap()
    dt_bias = dt("dt_bias", [16, 1], F32, kind="ExternalInput").ap()
    norm_w2 = dt("norm_w2", [128, 1], F32, kind="ExternalInput").ap()
    Wo = dt("Wo", [1024, D], F32, kind="ExternalInput").ap()

    gate_w = dt("gate_w", [E, D], F32, kind="ExternalInput").ap()
    w1 = dt("w_gate", [F, D], F32, kind="ExternalInput").ap()
    w3 = dt("w_up", [F, D], F32, kind="ExternalInput").ap()
    wd = dt("w_down", [D, F], F32, kind="ExternalInput").ap()
    A1 = dt("A1", [E * R, D], F32, kind="ExternalInput").ap()
    B1 = dt("B1", [E * F, R], F32, kind="ExternalInput").ap()
    A3 = dt("A3", [E * R, D], F32, kind="ExternalInput").ap()
    B3 = dt("B3", [E * F, R], F32, kind="ExternalInput").ap()
    A2 = dt("A2", [E * R, F], F32, kind="ExternalInput").ap()
    B2 = dt("B2", [E * D, R], F32, kind="ExternalInput").ap()

    cvals_in = dt("cvals", [128, 8], F32, kind="ExternalInput").ap()
    ident_in = dt("ident", [128, 128], F32, kind="ExternalInput").ap()
    idsh_in = dt("idsh", [128, 128], F32, kind="ExternalInput").ap()
    mask_in = dt("maskc", [128, 128], F32, kind="ExternalInput").ap()
    iota8_in = dt("iota8", [128, 8], F32, kind="ExternalInput").ap()
    ones_in = dt("ones", [128, 128], F32, kind="ExternalInput").ap()

    w1T = dt("w1T", [D, F], F32, kind="Internal").ap()
    w3T = dt("w3T", [D, F], F32, kind="Internal").ap()
    wdT = dt("wdT", [F, D], F32, kind="Internal").ap()
    B1Tg = dt("B1Tg", [2 * 128, F], F32, kind="Internal").ap()
    B3Tg = dt("B3Tg", [2 * 128, F], F32, kind="Internal").ap()
    A2T = dt("A2T", [F, 256], F32, kind="Internal").ap()
    xprime = dt("xprime", [Tloc, D], F32, kind="Internal").ap()

    p = Prog()
    with contextlib.ExitStack() as st:
        E_ = st.enter_context
        sems = {e: E_(nc.semaphore(f"s_{e}")) for e in Prog.ENGINES}

        cvals = E_(nc.sbuf_tensor("cvals_sb", [128, 8], F32))
        ident = E_(nc.sbuf_tensor("ident_sb", [128, 128], F32))
        idsh = E_(nc.sbuf_tensor("idsh_sb", [128, 128], F32))
        maskc = E_(nc.sbuf_tensor("mask_sb", [128, 128], F32))
        iota8 = E_(nc.sbuf_tensor("iota8_sb", [128, 8], F32))
        ones = E_(nc.sbuf_tensor("ones_sb", [128, 128], F32))
        nw2 = E_(nc.sbuf_tensor("nw2_sb", [128, 1], F32))
        A1Tg = [E_(nc.sbuf_tensor(f"A1Tg{g}", [128, 8 * 128], F32)) for g in range(2)]
        A3Tg = [E_(nc.sbuf_tensor(f"A3Tg{g}", [128, 8 * 128], F32)) for g in range(2)]
        B2Tg = [E_(nc.sbuf_tensor(f"B2Tg{g}", [128, 1024], F32)) for g in range(2)]
        gwT = E_(nc.sbuf_tensor("gwT", [128, 8 * 8], F32))

        p.dma("sync", cvals[:], cvals_in, writes=["cvals"])
        for _i, _v in enumerate(CONST_VALS):
            nc.const_aps.register(F32, _v, cvals[:, _i:_i + 1])
        p.dma("sync", ident[:], ident_in, writes=["ident"])
        p.dma("sync", idsh[:], idsh_in, writes=["idsh"])
        p.dma("sync", maskc[:], mask_in, writes=["maskc"])
        p.dma("sync", iota8[:], iota8_in, writes=["iota8"])
        p.dma("sync", ones[:], ones_in, writes=["ones"])
        p.dma("sync", nw2[:], norm_w2, writes=["nw2"])

        p0_tasks = []

        def task(fn):
            p0_tasks.append((None, fn))

        def task2(load_fn, compute_fn):
            p0_tasks.append((load_fn, compute_fn))

        with contextlib.ExitStack() as s0:
            E0 = s0.enter_context
            stgs = [E0(nc.sbuf_tensor(f"stg_{i}", [128, 1024], F32)) for i in range(2)]
            stg2s = [E0(nc.sbuf_tensor(f"stg2_{i}", [128, 1024], F32)) for i in range(1)]
            pts = [E0(nc.psum_tensor(f"pt{i}", [128, 512], F32)) for i in range(2)]

            def transpose_big(src_ap, dst_ap, rows, cols, tag):
                it = 0
                for r0 in range(0, rows, 128):
                    for cb0 in range(0, cols, 1024):
                        bi = it % 2
                        it += 1
                        def unit_load(bi=bi, r0=r0, cb0=cb0):
                            stg = stgs[bi]
                            sn = f"stg_{bi}"
                            w = min(1024, cols - cb0)
                            p.dma("sync", stg[:, 0:w], src_ap[r0:r0 + 128, cb0:cb0 + w],
                                  reads=[sn], writes=[sn])
                        def unit_compute(bi=bi, r0=r0, cb0=cb0):
                            stg, stg2, pt = stgs[bi], stg2s[0], pts[bi]
                            sn, s2n, ptn = f"stg_{bi}", "stg2_0", f"pt{bi}"
                            w = min(1024, cols - cb0)
                            for cb in range(w // 128):
                                def mkt(cb=cb, pt=pt, stg=stg):
                                    return lambda t: t.transpose(pt[:, 0:128], stg[:, cb * 128:(cb + 1) * 128], ident[:])
                                p.op("tensor", mkt(), reads=[sn, "ident"], writes=[ptn])
                                def mke(cb=cb, pt=pt, stg2=stg2):
                                    return lambda v: v.tensor_copy(stg2[:, cb * 128:(cb + 1) * 128], pt[:, 0:128])
                                p.op("vector", mke(), reads=[ptn], writes=[s2n])
                            for cb in range(w // 128):
                                p.dma("sync", dst_ap[cb0 + cb * 128: cb0 + (cb + 1) * 128, r0:r0 + 128],
                                      stg2[:, cb * 128:(cb + 1) * 128], reads=[s2n], writes=[f"d{tag}"])
                        task2(unit_load, unit_compute)

            transpose_big(w1, w1T, F, D, "w1")
            transpose_big(w3, w3T, F, D, "w3")
            transpose_big(wd, wdT, D, F, "wd")

            for (srcA, dstA, tagA) in ((A1, A1Tg, "a1"), (A3, A3Tg, "a3")):
                for g in range(2):
                    def unit(dstA=dstA, g=g, tagA=tagA):
                        p.op("gpsimd", lambda gp, d=dstA[g]: gp.memset(d[:], 0.0),
                             writes=[f"AT{tagA}{g}"])
                    task(unit)
                for e in range(E):
                    g, el = divmod(e, 4)
                    bi = e % 2
                    def unit(bi=bi, e=e, g=g, el=el, srcA=srcA, dstA=dstA, tagA=tagA):
                        stg, pt = stgs[bi], pts[bi]
                        sn, ptn = f"stg_{bi}", f"pt{bi}"
                        p.dma("sync", stg[0:16, 0:D], srcA[e * R:(e + 1) * R, :],
                              reads=[sn], writes=[sn])
                        for dc in range(8):
                            def mkt(dc=dc, pt=pt, stg=stg):
                                return lambda t: t.transpose(pt[:, 0:16], stg[0:16, dc * 128:(dc + 1) * 128],
                                                             ident[0:16, 0:16])
                            p.op("tensor", mkt(), reads=[sn, "ident"], writes=[ptn])
                            def mke(dc=dc, dstT=dstA[g], el=el, pt=pt):
                                return lambda v: v.tensor_copy(
                                    dstT[:, dc * 128 + el * 32: dc * 128 + el * 32 + 16], pt[:, 0:16])
                            p.op("vector", mke(), reads=[ptn], writes=[f"AT{tagA}{g}"])
                    task(unit)

            for g in range(2):
                def unit(g=g):
                    p.op("gpsimd", lambda gp, d=B2Tg[g]: gp.memset(d[:], 0.0), writes=[f"B2T{g}"])
                task(unit)
            for e in range(E):
                g, el = divmod(e, 4)
                bi = e % 2
                def unit(bi=bi, e=e, g=g, el=el):
                    stg, pt = stgs[bi], pts[bi]
                    sn, ptn = f"stg_{bi}", f"pt{bi}"
                    p.dma("sync", stg[:, 0:128].rearrange("p (dc r) -> p dc r", r=16),
                          B2[e * D: (e + 1) * D, :].rearrange("(dc p) r -> p dc r", p=128),
                          reads=[sn], writes=[sn])
                    for dc in range(8):
                        def mkt(dc=dc, pt=pt, stg=stg):
                            return lambda t: t.transpose(pt[0:16, 0:128], stg[:, dc * 16:(dc + 1) * 16], ident[:])
                        p.op("tensor", mkt(), reads=[sn, "ident"], writes=[ptn])
                        def mke(dc=dc, g=g, el=el, pt=pt):
                            return lambda v: v.tensor_copy(
                                B2Tg[g][el * 32: el * 32 + 16, dc * 128:(dc + 1) * 128], pt[0:16, 0:128])
                        p.op("vector", mke(), reads=[ptn], writes=[f"B2T{g}"])
                task(unit)

            for (srcB, dstB, tagB) in ((B1, B1Tg, "b1"), (B3, B3Tg, "b3")):
                for g in range(2):
                    for fb in range(F // 1024):
                        def unit(srcB=srcB, dstB=dstB, tagB=tagB, g=g, fb=fb):
                            p.op("gpsimd", lambda gp: gp.memset(stg2s[0][:], 0.0),
                                 reads=["stg2_0"], writes=["stg2_0"])
                            for el in range(4):
                                e = g * 4 + el
                                f0 = fb * 1024
                                bi = el % 2
                                stg, pt = stgs[bi], pts[bi]
                                sn, ptn = f"stg_{bi}", f"pt{bi}"
                                p.dma("sync", stg[:, 0:128].rearrange("p (fc r) -> p fc r", r=16),
                                      srcB[e * F + f0: e * F + f0 + 1024, :].rearrange("(fc p) r -> p fc r", p=128),
                                      reads=[sn], writes=[sn])
                                for fc in range(8):
                                    def mkt(fc=fc, pt=pt, stg=stg):
                                        return lambda t: t.transpose(pt[0:16, 0:128],
                                                                     stg[:, fc * 16:(fc + 1) * 16], ident[:])
                                    p.op("tensor", mkt(), reads=[sn, "ident"], writes=[ptn])
                                    def mke(el=el, fc=fc, pt=pt):
                                        return lambda v: v.tensor_copy(
                                            stg2s[0][el * 32: el * 32 + 16, fc * 128:(fc + 1) * 128], pt[0:16, 0:128])
                                    p.op("vector", mke(), reads=[ptn], writes=["stg2_0"])
                            p.dma("sync", dstB[g * 128:(g + 1) * 128, fb * 1024:(fb + 1) * 1024], stg2s[0][:],
                                  reads=["stg2_0"], writes=[f"dB{tagB}"])
                        task(unit)

            for fq in range(8):
                def unit(fq=fq):
                    p.op("gpsimd", lambda gp: gp.memset(stg2s[0][:, 0:1024], 0.0),
                         reads=["stg2_0"], writes=["stg2_0"])
                    for e in range(E):
                        g, el = divmod(e, 4)
                        bi = e % 2
                        stg, pt = stgs[bi], pts[bi]
                        sn, ptn = f"stg_{bi}", f"pt{bi}"
                        p.dma("sync", stg[0:16, 0:512], A2[e * R:(e + 1) * R, fq * 512:(fq + 1) * 512],
                              reads=[sn], writes=[sn])
                        for fc4 in range(4):
                            def mkt(fc4=fc4, pt=pt, stg=stg):
                                return lambda t: t.transpose(pt[:, 0:16], stg[0:16, fc4 * 128:(fc4 + 1) * 128],
                                                             ident[0:16, 0:16])
                            p.op("tensor", mkt(), reads=[sn, "ident"], writes=[ptn])
                            def mke(g=g, el=el, fc4=fc4, pt=pt):
                                return lambda v: v.tensor_copy(
                                    stg2s[0][:, fc4 * 256 + g * 128 + el * 32: fc4 * 256 + g * 128 + el * 32 + 16],
                                    pt[:, 0:16])
                            p.op("vector", mke(), reads=[ptn], writes=["stg2_0"])
                    for fc4 in range(4):
                        fc = fq * 4 + fc4
                        p.dma("sync", A2T[fc * 128:(fc + 1) * 128, :], stg2s[0][:, fc4 * 256:(fc4 + 1) * 256],
                              reads=["stg2_0"], writes=["dA2"])
                task(unit)
            def unit_gate():
                p.dma("sync", stgs[0][0:8, 0:D], gate_w, reads=["stg_0"], writes=["stg_0"])
                for dc in range(8):
                    def mkt(dc=dc):
                        return lambda t: t.transpose(pts[0][:, 0:8], stgs[0][0:8, dc * 128:(dc + 1) * 128],
                                                     ident[0:8, 0:8])
                    p.op("tensor", mkt(), reads=["stg_0", "ident"], writes=["pt0"])
                    def mke(dc=dc):
                        return lambda v: v.tensor_copy(gwT[:, dc * 8:(dc + 1) * 8], pts[0][:, 0:8])
                    p.op("vector", mke(), reads=["pt0"], writes=["gwT"])
            task(unit_gate)

            EA = s0.enter_context
            WqS = EA(nc.sbuf_tensor("WqS", [128, 8 * 1024], F32))
            WkS = EA(nc.sbuf_tensor("WkS", [128, 8 * 256], F32))
            WvS = EA(nc.sbuf_tensor("WvS", [128, 8 * 256], F32))
            WoS = EA(nc.sbuf_tensor("WoS", [128, 8 * 1024], F32))
            WaS = EA(nc.sbuf_tensor("WaS", [128, 8 * 16], F32))
            WalS = EA(nc.sbuf_tensor("WalS", [128, 8 * 16], F32))
            WbS = EA(nc.sbuf_tensor("WbS", [128, 8 * 4], F32))
            WcS = EA(nc.sbuf_tensor("WcS", [128, 12 * 4], F32))
            AlogS = EA(nc.sbuf_tensor("AlogS", [16, 1], F32))
            dtbS = EA(nc.sbuf_tensor("dtbS", [16, 1], F32))
            negeA = EA(nc.sbuf_tensor("negeA", [16, 1], F32))
            xc = [EA(nc.sbuf_tensor(f"xc{i}", [128, 1024], F32)) for i in range(2)]
            xn_fm = EA(nc.sbuf_tensor("xn_fm", [128, 8 * 128], F32))
            ring = EA(nc.sbuf_tensor("ring", [128, 12 * 131], F32))
            qkvs = EA(nc.sbuf_tensor("qkvs", [128, 12 * 128], F32))
            cacc = EA(nc.sbuf_tensor("cacc", [128, 2 * 128], F32))
            g_a = EA(nc.sbuf_tensor("g_a", [16, 128], F32))
            g_b = EA(nc.sbuf_tensor("g_b", [16, 128], F32))
            g_t1 = EA(nc.sbuf_tensor("g_t1", [16, 128], F32))
            g_t2 = EA(nc.sbuf_tensor("g_t2", [16, 128], F32))
            gamT = EA(nc.sbuf_tensor("gamT", [128, 16], F32))
            apreT = EA(nc.sbuf_tensor("apreT", [128, 16], F32))
            bpreT = EA(nc.sbuf_tensor("bpreT", [128, 4], F32))
            rowd8 = EA(nc.sbuf_tensor("rowd8", [1, 1024], F32))
            dmt_all = EA(nc.sbuf_tensor("dmt_all", [128, 16 * 128], F32))
            lamb_all = EA(nc.sbuf_tensor("lamb_all", [128, 16 * 128], F32))
            dms = EA(nc.sbuf_tensor("dms", [128, 256], F32))
            ms2 = EA(nc.sbuf_tensor("ms2", [2, 1024], F32))
            ap2 = EA(nc.sbuf_tensor("ap2", [2, 1024], F32))
            a2q = EA(nc.sbuf_tensor("a2q", [2, 1024], F32))
            r2 = EA(nc.sbuf_tensor("r2", [2, 256], F32))
            bsig = EA(nc.sbuf_tensor("bsig", [4, 128], F32))
            ones2 = EA(nc.sbuf_tensor("ones2", [128, 2], F32))
            vdup = EA(nc.sbuf_tensor("vdup", [64, 128], F32))
            kfm = [EA(nc.sbuf_tensor(f"kfm{i}", [128, 128], F32)) for i in range(4)]
            ktm = EA(nc.sbuf_tensor("ktm", [128, 4 * 64], F32))
            vtm = EA(nc.sbuf_tensor("vtm", [128, 4 * 64], F32))
            kwv = EA(nc.sbuf_tensor("kwv", [128, 128], F32))
            skk = EA(nc.sbuf_tensor("skk", [128, 8 * 64], F32))
            skv = EA(nc.sbuf_tensor("skv", [128, 8 * 64], F32))
            qs_b = EA(nc.sbuf_tensor("qs_b", [128, 128], F32))
            x1_b = EA(nc.sbuf_tensor("x1_b", [128, 128], F32))
            x2_b = EA(nc.sbuf_tensor("x2_b", [128, 128], F32))
            tt_b = EA(nc.sbuf_tensor("tt_b", [128, 128], F32))
            uu_b = EA(nc.sbuf_tensor("uu_b", [128, 128], F32))
            s_tmp = EA(nc.sbuf_tensor("s_tmp", [128, 128], F32))
            ocol = EA(nc.sbuf_tensor("ocol", [128, 8 * 128], F32))
            aout = EA(nc.sbuf_tensor("aout", [128, 1024], F32))
            sqs = EA(nc.sbuf_tensor("sqs", [128, 1024], F32))
            cols = EA(nc.sbuf_tensor("colsA", [128, 16], F32))
            rows = EA(nc.sbuf_tensor("rowsA", [1, 128], F32))
            P0 = EA(nc.psum_tensor("P0", [128, 512], F32))
            P1 = EA(nc.psum_tensor("P1", [128, 512], F32))
            P2 = EA(nc.psum_tensor("P2", [128, 512], F32))
            P3 = EA(nc.psum_tensor("P3", [128, 512], F32))
            P4 = EA(nc.psum_tensor("P4", [128, 512], F32))
            P5 = EA(nc.psum_tensor("P5", [128, 512], F32))
            blkA = EA(nc.Block())

            def fm_load(dst, src, name):
                c = src.shape[0] // 128
                o = src.shape[1]
                p.dma("sync", dst[:].rearrange("p (c o) -> p c o", c=c),
                      src.rearrange("(c p) o -> p c o", p=128), writes=[name])

            fm_load(WqS, Wq, "WqS")
            fm_load(WkS, Wk, "WkS")
            fm_load(WvS, Wv, "WvS")
            fm_load(WoS, Wo, "WoS")
            fm_load(WaS, Wa, "WaS")
            fm_load(WalS, Walpha, "WalS")
            fm_load(WbS, Wb, "WbS")
            fm_load(WcS, Wconv, "WcS")
            p.dma("sync", AlogS[:], A_log, writes=["AlogS"])
            p.dma("sync", dtbS[:], dt_bias, writes=["dtbS"])
            p.op("scalar", lambda s: s.activation(negeA[:], AlogS[:], AF.Exp),
                 reads=["AlogS"], writes=["negeA"])
            p.op("vector", lambda v: v.tensor_scalar(out=negeA[:], in0=negeA[:], scalar1=-1.0,
                                                     scalar2=None, op0=ALU.mult),
                 reads=["negeA"], writes=["negeA"])
            p.op("gpsimd", lambda g: g.memset(skk[:], 0.0), writes=["skk"])
            p.op("gpsimd", lambda g: g.memset(skv[:], 0.0), writes=["skv"])
            p.op("gpsimd", lambda g: g.memset(ring[:], 0.0), writes=["ring"])
            p.op("gpsimd", lambda g: g.memset(ones2[:], 0.0), writes=["ones2"])
            p.op("gpsimd", lambda g: g.memset(ones2[0:64, 0:1], 1.0), reads=["ones2"], writes=["ones2"])
            p.op("gpsimd", lambda g: g.memset(ones2[64:128, 1:2], 1.0), reads=["ones2"], writes=["ones2"])

            def ln_inline(src_buf, src_ap, dst_ap, dst_w, c0, sq, sqn, colb, coln):
                p.op("scalar", lambda s: s.activation(sq[:, 0:1024], src_ap, AF.Identity,
                                                      accum_out=colb[:, c0:c0 + 1]),
                     reads=[src_buf], writes=[sqn, coln])
                p.op("scalar", lambda s: s.activation(sq[:, 0:1024], src_ap, AF.Square,
                                                      accum_out=colb[:, c0 + 1:c0 + 2]),
                     reads=[src_buf], writes=[sqn, coln])
                p.op("vector", lambda v: v.tensor_scalar(out=colb[:, c0:c0 + 1], in0=colb[:, c0:c0 + 1],
                                                         scalar1=1.0 / D, scalar2=None, op0=ALU.mult),
                     reads=[coln], writes=[coln])
                p.op("scalar", lambda s: s.activation(colb[:, c0 + 2:c0 + 3], colb[:, c0:c0 + 1], AF.Square),
                     reads=[coln], writes=[coln])
                p.op("vector", lambda v: v.scalar_tensor_tensor(out=colb[:, c0 + 3:c0 + 4],
                                                                in0=colb[:, c0 + 1:c0 + 2],
                                                                scalar=1.0 / D, in1=colb[:, c0 + 2:c0 + 3],
                                                                op0=ALU.mult, op1=ALU.subtract),
                     reads=[coln], writes=[coln])
                p.op("scalar", lambda s: s.activation(colb[:, c0 + 4:c0 + 5], colb[:, c0 + 3:c0 + 4],
                                                      AF.Sqrt, bias=LN_EPS),
                     reads=[coln], writes=[coln])
                p.op("vector", lambda v: v.reciprocal(colb[:, c0 + 4:c0 + 5], colb[:, c0 + 4:c0 + 5]),
                     reads=[coln], writes=[coln])
                p.op("vector", lambda v: v.tensor_scalar(out=dst_ap, in0=src_ap,
                                                         scalar1=colb[:, c0:c0 + 1],
                                                         scalar2=colb[:, c0 + 4:c0 + 5],
                                                         op0=ALU.subtract, op1=ALU.mult),
                     reads=[src_buf, coln], writes=[dst_w])

            p0_state = {"i": 0, "pending": None}

            def pump():
                if p0_state["pending"] is not None:
                    p0_state["pending"]()
                    p0_state["pending"] = None
                i = p0_state["i"]
                if i < len(p0_tasks):
                    load_fn, compute_fn = p0_tasks[i]
                    p0_state["i"] = i + 1
                    if load_fn is None:
                        compute_fn()
                    else:
                        load_fn()
                        p0_state["pending"] = compute_fn

            def pump_rest():
                while p0_state["i"] < len(p0_tasks) or p0_state["pending"] is not None:
                    pump()

            for c in range(NCH):
                pump()
                xbuf = xc[c % 2]
                xnm = f"xc{c % 2}"
                p.dma("sync", xbuf[:], x_in[c * 128:(c + 1) * 128, :], reads=[xnm], writes=[xnm])
                ln_inline(xnm, xbuf[:], aout[:], "aout", 0, sqs, "sqs", cols, "colsA")
                for dc in range(8):
                    def mkt(dc=dc):
                        return lambda t: t.transpose(P5[:, 0:128], aout[:, dc * 128:(dc + 1) * 128], ident[:])
                    p.op("tensor", mkt(), reads=["aout", "ident"], writes=["P5"])
                    def mke(dc=dc):
                        return lambda v: v.tensor_copy(xn_fm[:, dc * 128:(dc + 1) * 128], P5[:, 0:128])
                    p.op("vector", mke(), reads=["P5"], writes=["xn_fm"])

                for t12 in range(12):
                    def mkc(t12=t12):
                        return lambda v: v.tensor_copy(ring[:, t12 * 131: t12 * 131 + 3],
                                                       ring[:, t12 * 131 + 128: t12 * 131 + 131])
                    p.op("vector", mkc(), reads=["ring"], writes=["ring"])

                mt = 0
                for (Wt, nmt, wn, wcols) in ((WqS, 8, "WqS", 1024), (WkS, 2, "WkS", 256), (WvS, 2, "WvS", 256)):
                    for m in range(nmt):
                        ps = (P0, P1, P2)[mt // 4]
                        psn = f"P{mt // 4}"
                        off = (mt % 4) * 128
                        for dc in range(8):
                            def mkm(Wt=Wt, m=m, dc=dc, ps=ps, off=off, wcols=wcols):
                                return lambda t: t.matmul(
                                    ps[:, off:off + 128],
                                    Wt[:, dc * wcols + m * 128: dc * wcols + m * 128 + 128],
                                    xn_fm[:, dc * 128:(dc + 1) * 128],
                                    start=(dc == 0), stop=(dc == 7))
                            p.op("tensor", mkm(), reads=[wn, "xn_fm"], writes=[psn])
                        mt += 1
                for t12 in range(12):
                    ps = (P0, P1, P2)[t12 // 4]
                    off = (t12 % 4) * 128
                    def mke(t12=t12, ps=ps, off=off):
                        return lambda v: v.tensor_copy(ring[:, t12 * 131 + 3: t12 * 131 + 131],
                                                       ps[:, off:off + 128])
                    p.op("vector", mke(), reads=[f"P{t12 // 4}"], writes=["ring"])

                pump()
                for (Wt, wn, off, m) in ((WaS, "WaS", 0, 16), (WalS, "WalS", 128, 16), (WbS, "WbS", 256, 4)):
                    for dc in range(8):
                        def mkm(Wt=Wt, dc=dc, off=off, m=m):
                            return lambda t: t.matmul(P3[0:m, off:off + 128],
                                                      Wt[:, dc * m: (dc + 1) * m],
                                                      xn_fm[:, dc * 128:(dc + 1) * 128],
                                                      start=(dc == 0), stop=(dc == 7))
                        p.op("tensor", mkm(), reads=[wn, "xn_fm"], writes=["P3"])

                for t12 in range(12):
                    cb = cacc[:, (t12 % 2) * 128:(t12 % 2) * 128 + 128]
                    def mk0(t12=t12, cb=cb):
                        return lambda v: v.tensor_scalar(out=cb, in0=ring[:, t12 * 131: t12 * 131 + 128],
                                                         scalar1=WcS[:, t12 * 4: t12 * 4 + 1],
                                                         scalar2=None, op0=ALU.mult)
                    p.op("vector", mk0(), reads=["ring", "WcS"], writes=["cacc"])
                    for j in range(1, 4):
                        def mkj(t12=t12, j=j, cb=cb):
                            return lambda v: v.scalar_tensor_tensor(
                                out=cb, in0=ring[:, t12 * 131 + j: t12 * 131 + j + 128],
                                scalar=WcS[:, t12 * 4 + j: t12 * 4 + j + 1],
                                in1=cb, op0=ALU.mult, op1=ALU.add)
                        p.op("vector", mkj(), reads=["ring", "WcS", "cacc"], writes=["cacc"])
                    def mks(t12=t12, cb=cb):
                        return lambda s: s.activation(qkvs[:, t12 * 128:(t12 + 1) * 128], cb, AF.Silu)
                    p.op("scalar", mks(), reads=["cacc"], writes=["qkvs"])

                p.op("scalar", lambda s: s.activation(g_a[:], P3[0:16, 0:128], AF.Identity, bias=dtbS[:]),
                     reads=["P3", "dtbS"], writes=["g_a"])
                p.op("scalar", lambda s: s.activation(g_t1[:], g_a[:], AF.Abs),
                     reads=["g_a"], writes=["g_t1"])
                p.op("scalar", lambda s: s.activation(g_t1[:], g_t1[:], AF.Exp, scale=-1.0),
                     reads=["g_t1"], writes=["g_t1"])
                p.op("scalar", lambda s: s.activation(g_t1[:], g_t1[:], AF.Ln, bias=1.0),
                     reads=["g_t1"], writes=["g_t1"])
                p.op("scalar", lambda s: s.activation(g_t2[:], g_a[:], AF.Relu),
                     reads=["g_a"], writes=["g_t2"])
                p.op("vector", lambda v: v.tensor_tensor(out=g_a[:], in0=g_t1[:], in1=g_t2[:], op=ALU.add),
                     reads=["g_t1", "g_t2"], writes=["g_a"])
                p.op("vector", lambda v: v.tensor_scalar(out=g_a[:], in0=g_a[:], scalar1=negeA[:],
                                                         scalar2=None, op0=ALU.mult),
                     reads=["g_a", "negeA"], writes=["g_a"])
                src, srcn, dst, dstn = g_a, "g_a", g_b, "g_b"
                sh = 1
                while sh < 128:
                    def mkcp(src=src, dst=dst, sh=sh):
                        return lambda v: v.tensor_copy(dst[:, 0:sh], src[:, 0:sh])
                    p.op("vector", mkcp(), reads=[srcn], writes=[dstn])
                    def mka(src=src, dst=dst, sh=sh):
                        return lambda v: v.tensor_tensor(out=dst[:, sh:128], in0=src[:, sh:128],
                                                         in1=src[:, 0:128 - sh], op=ALU.add)
                    p.op("vector", mka(), reads=[srcn], writes=[dstn])
                    src, srcn, dst, dstn = dst, dstn, src, srcn
                    sh *= 2
                gam, gamn = src, srcn

                p.op("tensor", lambda t: t.transpose(P4[:, 0:16], gam[:], ident[0:16, 0:16]),
                     reads=[gamn, "ident"], writes=["P4"])
                p.op("vector", lambda v: v.tensor_copy(gamT[:], P4[:, 0:16]),
                     reads=["P4"], writes=["gamT"])
                p.op("vector", lambda v: v.tensor_copy(g_t1[:], P3[0:16, 128:256]),
                     reads=["P3"], writes=["g_t1"])
                p.op("vector", lambda v: v.tensor_copy(g_t2[0:4, :], P3[0:4, 256:384]),
                     reads=["P3"], writes=["g_t2"])
                p.op("tensor", lambda t: t.transpose(P4[:, 16:32], g_t1[:], ident[0:16, 0:16]),
                     reads=["g_t1", "ident"], writes=["P4"])
                p.op("vector", lambda v: v.tensor_copy(apreT[:], P4[:, 16:32]),
                     reads=["P4"], writes=["apreT"])
                p.op("tensor", lambda t: t.transpose(P4[:, 32:36], g_t2[0:4, :], ident[0:4, 0:4]),
                     reads=["g_t2", "ident"], writes=["P4"])
                p.op("vector", lambda v: v.tensor_copy(bpreT[:], P4[:, 32:36]),
                     reads=["P4"], writes=["bpreT"])

                pump()
                p.op("scalar", lambda s: s.activation(sqs[:, 0:256], qkvs[:, 8 * 128:10 * 128], AF.Square),
                     reads=["qkvs"], writes=["sqs"])
                for t2 in range(2):
                    def mkssb(t2=t2):
                        return lambda t: t.matmul(P4[0:2, 128:256], ones2[:, 0:2],
                                                  sqs[:, t2 * 128:(t2 + 1) * 128],
                                                  start=True, stop=True)
                    p.op("tensor", mkssb(), reads=["sqs", "ones2"], writes=["P4"])
                    def mksse(t2=t2):
                        return lambda v: v.tensor_copy(r2[0:2, t2 * 128:(t2 + 1) * 128], P4[0:2, 128:256])
                    p.op("vector", mksse(), reads=["P4"], writes=["r2"])
                p.op("scalar", lambda s: s.activation(r2[:], r2[:], AF.Sqrt),
                     reads=["r2"], writes=["r2"])
                p.op("vector", lambda v: v.tensor_scalar(out=r2[:], in0=r2[:], scalar1=EPS,
                                                         scalar2=None, op0=ALU.add),
                     reads=["r2"], writes=["r2"])
                p.op("vector", lambda v: v.reciprocal(r2[:], r2[:]), reads=["r2"], writes=["r2"])

                p.op("tensor", lambda t: t.matmul(P5[0:4, 0:128], bpreT[:], ident[:],
                                                  start=True, stop=True),
                     reads=["bpreT", "ident"], writes=["P5"])
                p.op("scalar", lambda s: s.activation(bsig[:], P5[0:4, 0:128], AF.Sigmoid),
                     reads=["P5"], writes=["bsig"])
                for kv in range(4):
                    p.dma("sync", rowd8[0:1, kv * 128:(kv + 1) * 128],
                          r2[kv % 2: kv % 2 + 1, (kv // 2) * 128:(kv // 2 + 1) * 128],
                          reads=["r2"], writes=[f"rw{kv}"])
                    p.dma("sync", rowd8[0:1, 512 + kv * 128: 512 + (kv + 1) * 128],
                          bsig[kv:kv + 1, :], reads=["bsig"], writes=[f"rw{4 + kv}"])

                for kv in range(4):
                    tile = 8 + kv // 2
                    b = 64 * (kv % 2)
                    ob = 64 - b
                    kpre = qkvs[b:b + 64, tile * 128:(tile + 1) * 128]
                    vpre = qkvs[b:b + 64, (tile + 2) * 128:(tile + 3) * 128]
                    def mkbc(b=b, kv=kv):
                        tp = (0, b) if b else None
                        return lambda t: t.matmul(P4[b:b + 64, 256:384], ones[0:1, 0:64],
                                                  rowd8[0:1, kv * 128:(kv + 1) * 128],
                                                  start=True, stop=True, tile_position=tp)
                    p.op("tensor", mkbc(), reads=[f"rw{kv}", "ones"], writes=["P4"])
                    def mkkn(kv=kv, b=b, kpre=kpre):
                        return lambda v: v.tensor_tensor(out=kfm[kv][b:b + 64, 0:128], in0=kpre,
                                                         in1=P4[b:b + 64, 256:384], op=ALU.mult)
                    p.op("vector", mkkn(), reads=["qkvs", "P4"], writes=[f"kfm{kv}"])
                    def mkdup2(kv=kv, b=b, ob=ob):
                        st_ap = ident[0:64, 0:64] if b == 0 else idsh[64:128, 0:64]
                        return lambda t: t.matmul(P4[ob:ob + 64, 384:512], st_ap,
                                                  kfm[kv][b:b + 64, 0:128],
                                                  start=True, stop=True, tile_position=(b, ob))
                    p.op("tensor", mkdup2(), reads=[f"kfm{kv}", "ident", "idsh"], writes=["P4"])
                    def mkdupe(kv=kv, ob=ob):
                        return lambda v: v.tensor_copy(kfm[kv][ob:ob + 64, 0:128], P4[ob:ob + 64, 384:512])
                    p.op("vector", mkdupe(), reads=["P4"], writes=[f"kfm{kv}"])
                    def mkbb(b=b, kv=kv):
                        tp = (0, b) if b else None
                        return lambda t: t.matmul(P5[b:b + 64, 256:384], ones[0:1, 0:64],
                                                  rowd8[0:1, 512 + kv * 128: 512 + (kv + 1) * 128],
                                                  start=True, stop=True, tile_position=tp)
                    p.op("tensor", mkbb(), reads=[f"rw{4 + kv}", "ones"], writes=["P5"])
                    def mkvs(vpre=vpre, b=b):
                        return lambda v: v.tensor_tensor(out=vpre, in0=vpre, in1=P5[b:b + 64, 256:384],
                                                         op=ALU.mult)
                    p.op("vector", mkvs(), reads=["qkvs", "P5"], writes=["qkvs"])
                    if b == 0:
                        vsrc_ap = vpre
                        vsrc_n = "qkvs"
                    else:
                        def mkvd(vpre=vpre):
                            return lambda t: t.matmul(P4[0:64, 384:512], idsh[64:128, 0:64], vpre,
                                                      start=True, stop=True, tile_position=(64, 0))
                        p.op("tensor", mkvd(), reads=["qkvs", "idsh"], writes=["P4"])
                        p.op("vector", lambda v: v.tensor_copy(vdup[:], P4[0:64, 384:512]),
                             reads=["P4"], writes=["vdup"])
                        vsrc_ap = vdup[:]
                        vsrc_n = "vdup"
                    def mktk(kv=kv):
                        return lambda t: t.transpose(P5[:, 384:448], kfm[kv][0:64, 0:128], ident[0:64, 0:64])
                    p.op("tensor", mktk(), reads=[f"kfm{kv}", "ident"], writes=["P5"])
                    def mktke(kv=kv):
                        return lambda v: v.tensor_copy(ktm[:, kv * 64:(kv + 1) * 64], P5[:, 384:448])
                    p.op("vector", mktke(), reads=["P5"], writes=["ktm"])
                    def mktv(vsrc_ap=vsrc_ap):
                        return lambda t: t.transpose(P5[:, 448:512], vsrc_ap, ident[0:64, 0:64])
                    p.op("tensor", mktv(), reads=[vsrc_n, "ident"], writes=["P5"])
                    def mktve(kv=kv):
                        return lambda v: v.tensor_copy(vtm[:, kv * 64:(kv + 1) * 64], P5[:, 448:512])
                    p.op("vector", mktve(), reads=["P5"], writes=["vtm"])

                for hw8 in range(2):
                    for hh in range(8):
                        h = hw8 * 8 + hh
                        p.dma("sync", rowd8[0:1, hh * 128:(hh + 1) * 128], gam[h:h + 1, :],
                              reads=[gamn], writes=[f"rw{hh}"])
                    for hh in range(8):
                        h = hw8 * 8 + hh
                        par = h % 2
                        bankB, bn = ((P0, "P0"), (P3, "P3"))[par]
                        def mkgb(hh=hh, bankB=bankB):
                            return lambda t: t.matmul(bankB[:, 0:128], ones[0:1, 0:128],
                                                      rowd8[0:1, hh * 128:(hh + 1) * 128],
                                                      start=True, stop=True)
                        p.op("tensor", mkgb(), reads=[f"rw{hh}", "ones"], writes=[bn])
                        def mkd0(h=h, bankB=bankB):
                            return lambda v: v.scalar_tensor_tensor(
                                out=dmt_all[:, h * 128:(h + 1) * 128], in0=bankB[:, 0:128],
                                scalar=gamT[:, h:h + 1], in1=maskc[:], op0=ALU.subtract, op1=ALU.add)
                        p.op("vector", mkd0(), reads=[bn, "gamT", "maskc"], writes=["dmt_all"])
                        def mklc(h=h, bankB=bankB):
                            return lambda v: v.tensor_copy(lamb_all[:, h * 128:(h + 1) * 128], bankB[:, 0:128])
                        p.op("vector", mklc(), reads=[bn], writes=["lamb_all"])
                p.op("scalar", lambda s: s.activation(dmt_all[:], dmt_all[:], AF.Exp),
                     reads=["dmt_all"], writes=["dmt_all"])
                p.op("scalar", lambda s: s.activation(lamb_all[:], lamb_all[:], AF.Exp),
                     reads=["lamb_all"], writes=["lamb_all"])

                def head_ctx(h):
                    par = h % 2
                    return dict(
                        par=par, b=64 * par, kv=h // 4, qt=h // 2, h0=h * 128,
                        bankW=((P1, "P1"), (P4, "P4"))[par],
                        bankS=((P2, "P2"), (P5, "P5"))[par],
                        dmn=f"dms{par}", dm_ap=dms[:, par * 128:(par + 1) * 128])

                for hp in range(8):
                    pump()
                    ctxs = [head_ctx(2 * hp), head_ctx(2 * hp + 1)]
                    for C in ctxs:
                        b, qt = C["b"], C["qt"]
                        q_ap = qkvs[b:b + 64, qt * 128:(qt + 1) * 128]
                        def mkqs(q_ap=q_ap, b=b):
                            return lambda v: v.tensor_scalar(out=qs_b[b:b + 64, :], in0=q_ap, scalar1=0.125,
                                                             scalar2=None, op0=ALU.mult)
                        p.op("vector", mkqs(), reads=["qkvs"], writes=[f"qs{b}"])
                    for app in range(3):
                        zsel = ((qs_b, "qs"), (x1_b, "x1"), (x2_b, "x2"))[app]
                        Ssel = (("skk",), ("skk",), ("skv",))[app][0]
                        vsel = ((ktm, "ktm"), (ktm, "ktm"), (vtm, "vtm"))[app]
                        for C in ctxs:
                            b, kv, h = C["b"], C["kv"], C["qt"] * 2 + C["par"]
                            bankW, wn_ = C["bankW"]
                            bankS, sn_ = C["bankS"]
                            zbuf, znb = zsel
                            zn = f"{znb}{b}"
                            z_ap = zbuf[b:b + 64, 0:128]
                            Sbuf = skk if Ssel == "skk" else skv
                            S_ap = Sbuf[b:b + 64, (h // 2) * 64:(h // 2 + 1) * 64]
                            def mkzs(S_ap=S_ap, z_ap=z_ap, b=b, bankW=bankW):
                                tp = (b, b) if b else None
                                return lambda t: t.matmul(bankW[b:b + 64, 0:128], S_ap, z_ap,
                                                          start=True, stop=True, tile_position=tp)
                            p.op("tensor", mkzs(), reads=[Ssel, zn], writes=[wn_])
                            def mksc(kv=kv, b=b, z_ap=z_ap, bankS=bankS):
                                tp = (b, 0) if b else None
                                return lambda t: t.matmul(bankS[:, 0:128], kfm[kv][b:b + 64, 0:128], z_ap,
                                                          start=True, stop=True, tile_position=tp)
                            p.op("tensor", mksc(), reads=[f"kfm{kv}", zn], writes=[sn_])
                            def mkdm(h0=C["h0"], bankS=bankS, dm_ap=C["dm_ap"]):
                                return lambda v: v.tensor_tensor(out=dm_ap, in0=bankS[:, 0:128],
                                                                 in1=dmt_all[:, h0:h0 + 128], op=ALU.mult)
                            p.op("vector", mkdm(), reads=[sn_, "dmt_all"], writes=[C["dmn"]])
                        for C in ctxs:
                            b, kv = C["b"], C["kv"]
                            bankW, wn_ = C["bankW"]
                            vside, vsn = vsel
                            def mkin(vside=vside, kv=kv, b=b, bankW=bankW, dm_ap=C["dm_ap"]):
                                tp = (0, b) if b else None
                                return lambda t: t.matmul(bankW[b:b + 64, 128:256],
                                                          vside[:, kv * 64:(kv + 1) * 64],
                                                          dm_ap, start=True, stop=True, tile_position=tp)
                            p.op("tensor", mkin(), reads=[vsn, C["dmn"]], writes=[wn_])
                        for C in ctxs:
                            b, h0, qt = C["b"], C["h0"], C["qt"]
                            bankW, wn_ = C["bankW"]
                            def mktt2(b=b, bankW=bankW, h0=h0):
                                return lambda v: v.tensor_tensor(out=tt_b[b:b + 64, :],
                                                                 in0=bankW[b:b + 64, 0:128],
                                                                 in1=lamb_all[b:b + 64, h0:h0 + 128],
                                                                 op=ALU.mult)
                            p.op("vector", mktt2(), reads=[wn_, "lamb_all"], writes=[f"tt{b}"])
                            if app < 2:
                                nxt, nxb = ((x1_b, "x1"), (x2_b, "x2"))[app]
                                def mkuu(b=b, bankW=bankW):
                                    return lambda v: v.tensor_tensor(out=uu_b[b:b + 64, :],
                                                                     in0=tt_b[b:b + 64, :],
                                                                     in1=bankW[b:b + 64, 128:256], op=ALU.add)
                                p.op("vector", mkuu(), reads=[f"tt{b}", wn_], writes=[f"uu{b}"])
                                def mkx(nxt=nxt, b=b):
                                    return lambda v: v.scalar_tensor_tensor(out=nxt[b:b + 64, :],
                                                                            in0=uu_b[b:b + 64, :], scalar=-0.25,
                                                                            in1=qs_b[b:b + 64, :],
                                                                            op0=ALU.mult, op1=ALU.add)
                                p.op("vector", mkx(), reads=[f"uu{b}", f"qs{b}"], writes=[f"{nxb}{b}"])
                            else:
                                def mkuo(b=b, bankW=bankW, qt=qt):
                                    return lambda v: v.tensor_tensor(
                                        out=ocol[b:b + 64, qt * 128:(qt + 1) * 128],
                                        in0=tt_b[b:b + 64, :],
                                        in1=bankW[b:b + 64, 128:256], op=ALU.add)
                                p.op("vector", mkuo(), reads=[f"tt{b}", wn_], writes=["ocol"])
                    for C in ctxs:
                        par, kv, h0 = C["par"], C["kv"], C["h0"]
                        def mkkw(kv=kv, par=par, h0=h0):
                            return lambda v: v.tensor_scalar(out=kwv[:, par * 64:(par + 1) * 64],
                                                             in0=ktm[:, kv * 64:(kv + 1) * 64],
                                                             scalar1=dmt_all[:, h0 + 127:h0 + 128],
                                                             scalar2=None, op0=ALU.mult)
                        p.op("vector", mkkw(), reads=["ktm", "dmt_all"], writes=[f"kw{par}"])
                    for (Sn, msn) in (("skk", "ktm"), ("skv", "vtm")):
                        msrc = ktm if msn == "ktm" else vtm
                        Sbuf = skk if Sn == "skk" else skv
                        for C in ctxs:
                            b, kv, par = C["b"], C["kv"], C["par"]
                            bankW, wn_ = C["bankW"]
                            def mkds(msrc=msrc, kv=kv, b=b, par=par, bankW=bankW):
                                tp = (0, b) if b else None
                                return lambda t: t.matmul(bankW[b:b + 64, 256:320],
                                                          kwv[:, par * 64:(par + 1) * 64],
                                                          msrc[:, kv * 64:(kv + 1) * 64],
                                                          start=True, stop=True, tile_position=tp)
                            p.op("tensor", mkds(), reads=[f"kw{par}", msn], writes=[wn_])
                        for C in ctxs:
                            b, par, h0, qt = C["b"], C["par"], C["h0"], C["qt"]
                            bankW, wn_ = C["bankW"]
                            h = qt * 2 + par
                            S_ap = Sbuf[b:b + 64, (h // 2) * 64:(h // 2 + 1) * 64]
                            def mks1(S_ap=S_ap, b=b, par=par, h0=h0):
                                return lambda v: v.tensor_scalar(out=s_tmp[b:b + 64, par * 64:(par + 1) * 64],
                                                                 in0=S_ap,
                                                                 scalar1=lamb_all[b:b + 64, h0 + 127:h0 + 128],
                                                                 scalar2=None, op0=ALU.mult)
                            p.op("vector", mks1(), reads=[Sn, "lamb_all"], writes=[f"st{par}"])
                            def mks2(S_ap=S_ap, b=b, par=par, bankW=bankW):
                                return lambda v: v.tensor_tensor(out=S_ap,
                                                                 in0=s_tmp[b:b + 64, par * 64:(par + 1) * 64],
                                                                 in1=bankW[b:b + 64, 256:320], op=ALU.add)
                            p.op("vector", mks2(), reads=[f"st{par}", wn_], writes=[Sn])

                pump()
                p.op("scalar", lambda s: s.activation(sqs[:, 0:1024], ocol[:], AF.Square),
                     reads=["ocol"], writes=["sqs"])
                for slot in range(8):
                    bankW, wn_ = ((P1, "P1"), (P4, "P4"))[slot % 2]
                    def mkmm(slot=slot, bankW=bankW):
                        return lambda t: t.matmul(bankW[0:2, 320:448], ones2[:, 0:2],
                                                  sqs[:, slot * 128:(slot + 1) * 128],
                                                  start=True, stop=True)
                    p.op("tensor", mkmm(), reads=["sqs", "ones2"], writes=[wn_])
                    def mkme(slot=slot, bankW=bankW):
                        return lambda v: v.tensor_copy(ms2[0:2, slot * 128:(slot + 1) * 128],
                                                       bankW[0:2, 320:448])
                    p.op("vector", mkme(), reads=[wn_], writes=["ms2"])
                    bankS, sn_ = ((P2, "P2"), (P5, "P5"))[slot % 2]
                    def mkax(slot=slot, bankS=bankS):
                        return lambda t: t.matmul(bankS[0:2, 128:256], apreT[:, 2 * slot: 2 * slot + 2],
                                                  ident[:], start=True, stop=True)
                    p.op("tensor", mkax(), reads=["apreT", "ident"], writes=[sn_])
                    def mkaxe(slot=slot, bankS=bankS):
                        return lambda v: v.tensor_copy(ap2[0:2, slot * 128:(slot + 1) * 128],
                                                       bankS[0:2, 128:256])
                    p.op("vector", mkaxe(), reads=[sn_], writes=["ap2"])
                p.op("scalar", lambda s: s.activation(ap2[:], ap2[:], AF.Sigmoid),
                     reads=["ap2"], writes=["ap2"])
                p.op("scalar", lambda s: s.activation(a2q[:], ap2[:], AF.Square),
                     reads=["ap2"], writes=["a2q"])
                p.op("vector", lambda v: v.tensor_tensor(out=ms2[:], in0=ms2[:], in1=a2q[:],
                                                         op=ALU.mult),
                     reads=["ms2", "a2q"], writes=["ms2"])
                p.op("scalar", lambda s: s.activation(ms2[:], ms2[:], AF.Sqrt,
                                                      scale=1.0 / 64, bias=EPS),
                     reads=["ms2"], writes=["ms2"])
                p.op("vector", lambda v: v.reciprocal(ms2[:], ms2[:]),
                     reads=["ms2"], writes=["ms2"])
                p.op("vector", lambda v: v.tensor_tensor(out=a2q[:], in0=ms2[:], in1=ap2[:],
                                                         op=ALU.mult),
                     reads=["ms2", "ap2"], writes=["a2q"])
                for hw8 in range(2):
                    for hh in range(8):
                        h = hw8 * 8 + hh
                        p.dma("sync", rowd8[0:1, hh * 128:(hh + 1) * 128],
                              a2q[h % 2: h % 2 + 1, (h // 2) * 128:(h // 2 + 1) * 128],
                              reads=["a2q"], writes=[f"rw{hh}"])
                    for hh in range(8):
                        h = hw8 * 8 + hh
                        par = h % 2
                        b = 64 * par
                        bankB, bn = ((P0, "P0"), (P3, "P3"))[par]
                        def mkub(b=b, hh=hh, bankB=bankB):
                            tp = (0, b) if b else None
                            return lambda t: t.matmul(bankB[b:b + 64, 384:512], ones[0:1, 0:64],
                                                      rowd8[0:1, hh * 128:(hh + 1) * 128],
                                                      start=True, stop=True, tile_position=tp)
                        p.op("tensor", mkub(), reads=[f"rw{hh}", "ones"], writes=[bn])
                        def mkof(h=h, b=b, bankB=bankB):
                            return lambda v: v.scalar_tensor_tensor(
                                out=ocol[b:b + 64, (h // 2) * 128:(h // 2) * 128 + 128],
                                in0=ocol[b:b + 64, (h // 2) * 128:(h // 2) * 128 + 128],
                                scalar=nw2[b:b + 64, 0:1],
                                in1=bankB[b:b + 64, 384:512], op0=ALU.mult, op1=ALU.mult)
                        p.op("vector", mkof(), reads=["ocol", "nw2", bn], writes=["ocol"])

                for nh in range(2):
                    for dc in range(8):
                        def mkwo(nh=nh, dc=dc):
                            ps = (P0, P1)[nh]
                            return lambda t: t.matmul(ps[:, 0:512],
                                                      ocol[:, dc * 128:(dc + 1) * 128],
                                                      WoS[:, dc * 1024 + nh * 512: dc * 1024 + nh * 512 + 512],
                                                      start=(dc == 0), stop=(dc == 7))
                        p.op("tensor", mkwo(), reads=["ocol", "WoS"], writes=[("P0", "P1")[nh]])
                p.op("scalar", lambda s: s.activation(aout[:, 0:512], P0[:, 0:512], AF.Identity,
                                                      accum_out=cols[:, 8:9]),
                     reads=["P0"], writes=["aout", "colsA"])
                p.op("scalar", lambda s: s.activation(aout[:, 512:1024], P1[:, 0:512], AF.Identity,
                                                      accum_out=cols[:, 9:10]),
                     reads=["P1"], writes=["aout", "colsA"])
                p.op("vector", lambda v: v.tensor_tensor(out=cols[:, 8:9], in0=cols[:, 8:9],
                                                         in1=cols[:, 9:10], op=ALU.add),
                     reads=["colsA"], writes=["colsA"])
                p.op("scalar", lambda s: s.activation(sqs[:, 0:1024], aout[:, 0:1024], AF.Square,
                                                      accum_out=cols[:, 10:11]),
                     reads=["aout"], writes=["sqs", "colsA"])
                p.op("vector", lambda v: v.tensor_scalar(out=cols[:, 8:9], in0=cols[:, 8:9],
                                                         scalar1=1.0 / D, scalar2=None, op0=ALU.mult),
                     reads=["colsA"], writes=["colsA"])
                p.op("scalar", lambda s: s.activation(cols[:, 11:12], cols[:, 8:9], AF.Square),
                     reads=["colsA"], writes=["colsA"])
                p.op("vector", lambda v: v.scalar_tensor_tensor(out=cols[:, 12:13], in0=cols[:, 10:11],
                                                                scalar=1.0 / D, in1=cols[:, 11:12],
                                                                op0=ALU.mult, op1=ALU.subtract),
                     reads=["colsA"], writes=["colsA"])
                p.op("scalar", lambda s: s.activation(cols[:, 13:14], cols[:, 12:13], AF.Sqrt, bias=LN_EPS),
                     reads=["colsA"], writes=["colsA"])
                p.op("vector", lambda v: v.reciprocal(cols[:, 13:14], cols[:, 13:14]),
                     reads=["colsA"], writes=["colsA"])
                p.op("vector", lambda v: v.tensor_scalar(out=aout[:, 0:1024], in0=aout[:, 0:1024],
                                                         scalar1=cols[:, 8:9], scalar2=cols[:, 13:14],
                                                         op0=ALU.subtract, op1=ALU.mult),
                     reads=["aout", "colsA"], writes=["aout"])
                def mkres(xbuf=xbuf, xnm=xnm):
                    return lambda v: v.tensor_tensor(out=aout[:, 0:1024], in0=aout[:, 0:1024],
                                                     in1=xbuf[:], op=ALU.add)
                p.op("vector", mkres(), reads=["aout", xnm], writes=["aout"])
                p.dma("sync", xprime[c * 128:(c + 1) * 128, :], aout[:, 0:1024],
                      reads=["aout"], writes=["xprime_hbm"])

            pump_rest()
            p.emit(blkA, sems, start=0)
        mB = p.new_phase()
        with contextlib.ExitStack() as sB:
            EB = sB.enter_context
            xpt = EB(nc.sbuf_tensor("xpt", [128, 4 * 1024], F32))
            xntm = EB(nc.sbuf_tensor("xntm", [128, 1024], F32))
            xnfm = EB(nc.sbuf_tensor("xnfm", [128, 8 * 512], F32))
            w1s = [EB(nc.sbuf_tensor(f"w1s{i}", [128, 8 * FS], F32)) for i in range(2)]
            w3s = [EB(nc.sbuf_tensor(f"w3s{i}", [128, 8 * FS], F32)) for i in range(2)]
            wds = [EB(nc.sbuf_tensor(f"wds{i}", [128, (FS // 128) * 1024], F32)) for i in range(2)]
            b1s = [EB(nc.sbuf_tensor(f"b1s{i}", [128, 2 * FS], F32)) for i in range(2)]
            b3s = [EB(nc.sbuf_tensor(f"b3s{i}", [128, 2 * FS], F32)) for i in range(2)]
            a2s = [EB(nc.sbuf_tensor(f"a2s{i}", [128, (FS // 128) * 256], F32)) for i in range(2)]
            c1sb = EB(nc.sbuf_tensor("c1sb", [128, 2 * 512], F32))
            c3sb = EB(nc.sbuf_tensor("c3sb", [128, 2 * 512], F32))
            t1sb = EB(nc.sbuf_tensor("t1sb", [128, 3 * 512], F32))
            s1sb = EB(nc.sbuf_tensor("s1sb", [128, 3 * 512], F32))
            t3sb = EB(nc.sbuf_tensor("t3sb", [128, 3 * 512], F32))
            accf = EB(nc.sbuf_tensor("accf", [128, 2 * 512], F32))
            h1sb = EB(nc.sbuf_tensor("h1sb", [128, 2 * 512], F32))
            h3sb = EB(nc.sbuf_tensor("h3sb", [128, 2 * 512], F32))
            w8big = EB(nc.sbuf_tensor("w8big", [128, 8 * 512], F32))
            zsb = EB(nc.sbuf_tensor("zsb", [128, 2 * 512], F32))
            acctm = EB(nc.sbuf_tensor("acctm", [128, 4 * 1024], F32))
            lgt = EB(nc.sbuf_tensor("lgt", [128, 4 * 8], F32))
            rt = EB(nc.sbuf_tensor("rt", [128, 64], F32))
            w8tm = EB(nc.sbuf_tensor("w8tm", [128, 4 * 8], F32))
            outsb = EB(nc.sbuf_tensor("outsb", [128, 1024], F32))
            colsB = EB(nc.sbuf_tensor("colsB", [128, 16], F32))
            PD0 = EB(nc.psum_tensor("PD0", [128, 512], F32))
            PD1 = EB(nc.psum_tensor("PD1", [128, 512], F32))
            PD2 = EB(nc.psum_tensor("PD2", [128, 512], F32))
            PD3 = EB(nc.psum_tensor("PD3", [128, 512], F32))
            PW = EB(nc.psum_tensor("PW", [128, 1024], F32))
            PZ = EB(nc.psum_tensor("PZ", [128, 512], F32))
            blkB = EB(nc.Block())

            def load_stripe(s, bi):
                sl = slice(s * FS, (s + 1) * FS)
                p.dma("sync", w1s[bi][:].rearrange("p (c f) -> p c f", c=8),
                      w1T[0:D, sl].rearrange("(c p) f -> p c f", p=128),
                      reads=[f"w1s{bi}"], writes=[f"w1s{bi}"])
                p.dma("sync", w3s[bi][:].rearrange("p (c f) -> p c f", c=8),
                      w3T[0:D, sl].rearrange("(c p) f -> p c f", p=128),
                      reads=[f"w3s{bi}"], writes=[f"w3s{bi}"])
                p.dma("sync", wds[bi][:].rearrange("p (c d) -> p c d", c=FS // 128),
                      wdT[sl, :].rearrange("(c p) d -> p c d", p=128),
                      reads=[f"wds{bi}"], writes=[f"wds{bi}"])
                p.dma("sync", b1s[bi][:, 0:FS], B1Tg[0:128, sl], reads=[f"b1s{bi}"], writes=[f"b1s{bi}"])
                p.dma("sync", b1s[bi][:, FS:2 * FS], B1Tg[128:256, sl], reads=[f"b1s{bi}"], writes=[f"b1s{bi}"])
                p.dma("sync", b3s[bi][:, 0:FS], B3Tg[0:128, sl], reads=[f"b3s{bi}"], writes=[f"b3s{bi}"])
                p.dma("sync", b3s[bi][:, FS:2 * FS], B3Tg[128:256, sl], reads=[f"b3s{bi}"], writes=[f"b3s{bi}"])
                p.dma("sync", a2s[bi][:].rearrange("p (c d) -> p c d", c=FS // 128),
                      A2T[sl, :].rearrange("(c p) d -> p c d", p=128),
                      reads=[f"a2s{bi}"], writes=[f"a2s{bi}"])

            for tt in range(NTT):
                t0 = tt * TT_TOK
                for sub in range(NSUB):
                    p.dma("sync", xpt[:, sub * 1024:(sub + 1) * 1024],
                          xprime[t0 + sub * 128: t0 + (sub + 1) * 128, :],
                          reads=["xpt", "xprime_hbm"], writes=["xpt"])
                for sub in range(NSUB):
                    src = xpt[:, sub * 1024:(sub + 1) * 1024]
                    dst = xntm[:, 0:1024]
                    p.op("scalar", lambda s, src=src: s.activation(t3sb[:, 0:1024], src, AF.Identity,
                                                                   accum_out=colsB[:, 0:1]),
                         reads=["xpt"], writes=["t3sb", "colsB"])
                    p.op("scalar", lambda s, src=src: s.activation(t3sb[:, 0:1024], src, AF.Square,
                                                                   accum_out=colsB[:, 1:2]),
                         reads=["xpt"], writes=["t3sb", "colsB"])
                    p.op("vector", lambda v: v.tensor_scalar(out=colsB[:, 0:1], in0=colsB[:, 0:1],
                                                             scalar1=1.0 / D, scalar2=None, op0=ALU.mult),
                         reads=["colsB"], writes=["colsB"])
                    p.op("scalar", lambda s: s.activation(colsB[:, 2:3], colsB[:, 0:1], AF.Square),
                         reads=["colsB"], writes=["colsB"])
                    p.op("vector", lambda v: v.scalar_tensor_tensor(out=colsB[:, 3:4], in0=colsB[:, 1:2],
                                                                    scalar=1.0 / D, in1=colsB[:, 2:3],
                                                                    op0=ALU.mult, op1=ALU.subtract),
                         reads=["colsB"], writes=["colsB"])
                    p.op("scalar", lambda s: s.activation(colsB[:, 4:5], colsB[:, 3:4], AF.Sqrt, bias=LN_EPS),
                         reads=["colsB"], writes=["colsB"])
                    p.op("vector", lambda v: v.reciprocal(colsB[:, 4:5], colsB[:, 4:5]),
                         reads=["colsB"], writes=["colsB"])
                    p.op("vector", lambda v, src=src, dst=dst: v.tensor_scalar(
                        out=dst, in0=src, scalar1=colsB[:, 0:1], scalar2=colsB[:, 4:5],
                        op0=ALU.subtract, op1=ALU.mult),
                         reads=["xpt", "colsB"], writes=["xntm"])
                    for dc in range(8):
                        def mkt(sub=sub, dc=dc):
                            return lambda t: t.transpose(
                                PD0[:, 0:128], xntm[:, dc * 128:(dc + 1) * 128], ident[:])
                        p.op("tensor", mkt(), reads=["xntm", "ident"], writes=["PD0"])
                        def mke(sub=sub, dc=dc):
                            return lambda v: v.tensor_copy(
                                xnfm[:, dc * TT_TOK + sub * 128: dc * TT_TOK + (sub + 1) * 128], PD0[:, 0:128])
                        p.op("vector", mke(), reads=["PD0"], writes=["xnfm"])

                for dc in range(8):
                    def mkg(dc=dc):
                        return lambda t: t.matmul(PD1[0:8, 0:TT_TOK], gwT[:, dc * 8:(dc + 1) * 8],
                                                  xnfm[:, dc * TT_TOK:(dc + 1) * TT_TOK],
                                                  start=(dc == 0), stop=(dc == 7))
                    p.op("tensor", mkg(), reads=["gwT", "xnfm"], writes=["PD1"])
                p.op("vector", lambda v: v.tensor_copy(zsb[0:8, 0:TT_TOK], PD1[0:8, 0:TT_TOK]),
                     reads=["PD1"], writes=["zsb"])
                for sub in range(NSUB):
                    def mktl(sub=sub):
                        return lambda t: t.transpose(PD2[:, 0:8], zsb[0:8, sub * 128:(sub + 1) * 128],
                                                     ident[0:8, 0:8])
                    p.op("tensor", mktl(), reads=["zsb", "ident"], writes=["PD2"])
                    def mkel(sub=sub):
                        return lambda v: v.tensor_copy(lgt[:, sub * 8:(sub + 1) * 8], PD2[:, 0:8])
                    p.op("vector", mkel(), reads=["PD2"], writes=["lgt"])

                for sub in range(NSUB):
                    lg_ap = lgt[:, sub * 8:(sub + 1) * 8]
                    p.op("vector", lambda v, lg_ap=lg_ap: v.tensor_reduce(out=rt[:, 0:1], in_=lg_ap,
                                                                          axis=AX.X, op=ALU.max),
                         reads=["lgt"], writes=["rt"])
                    p.op("vector", lambda v, lg_ap=lg_ap: v.tensor_scalar(out=rt[:, 16:24], in0=lg_ap,
                                                                          scalar1=rt[:, 0:1], scalar2=None,
                                                                          op0=ALU.is_equal),
                         reads=["lgt", "rt"], writes=["rt"])
                    p.op("vector", lambda v: v.scalar_tensor_tensor(out=rt[:, 16:24], in0=rt[:, 16:24],
                                                                    scalar=-1024.0, in1=iota8[:],
                                                                    op0=ALU.mult, op1=ALU.add),
                         reads=["rt", "iota8"], writes=["rt"])
                    p.op("vector", lambda v: v.tensor_reduce(out=rt[:, 1:2], in_=rt[:, 16:24],
                                                             axis=AX.X, op=ALU.min),
                         reads=["rt"], writes=["rt"])
                    p.op("vector", lambda v: v.tensor_scalar(out=rt[:, 16:24], in0=rt[:, 16:24],
                                                             scalar1=rt[:, 1:2], scalar2=None,
                                                             op0=ALU.is_equal),
                         reads=["rt"], writes=["rt"])
                    p.op("vector", lambda v, lg_ap=lg_ap: v.scalar_tensor_tensor(
                        out=rt[:, 24:32], in0=rt[:, 16:24], scalar=NEG, in1=lg_ap,
                        op0=ALU.mult, op1=ALU.add),
                         reads=["rt", "lgt"], writes=["rt"])
                    p.op("vector", lambda v: v.tensor_reduce(out=rt[:, 2:3], in_=rt[:, 24:32],
                                                             axis=AX.X, op=ALU.max),
                         reads=["rt"], writes=["rt"])
                    p.op("vector", lambda v: v.tensor_scalar(out=rt[:, 32:40], in0=rt[:, 24:32],
                                                             scalar1=rt[:, 2:3], scalar2=None,
                                                             op0=ALU.is_equal),
                         reads=["rt"], writes=["rt"])
                    p.op("vector", lambda v: v.scalar_tensor_tensor(out=rt[:, 32:40], in0=rt[:, 32:40],
                                                                    scalar=-1024.0, in1=iota8[:],
                                                                    op0=ALU.mult, op1=ALU.add),
                         reads=["rt", "iota8"], writes=["rt"])
                    p.op("vector", lambda v: v.tensor_reduce(out=rt[:, 3:4], in_=rt[:, 32:40],
                                                             axis=AX.X, op=ALU.min),
                         reads=["rt"], writes=["rt"])
                    p.op("vector", lambda v: v.tensor_scalar(out=rt[:, 32:40], in0=rt[:, 32:40],
                                                             scalar1=rt[:, 3:4], scalar2=None,
                                                             op0=ALU.is_equal),
                         reads=["rt"], writes=["rt"])
                    p.op("vector", lambda v: v.tensor_scalar(out=rt[:, 4:5], in0=rt[:, 0:1], scalar1=-1.0,
                                                             scalar2=None, op0=ALU.mult),
                         reads=["rt"], writes=["rt"])
                    p.op("scalar", lambda s: s.activation(rt[:, 5:6], rt[:, 2:3], AF.Sigmoid, bias=rt[:, 4:5]),
                         reads=["rt"], writes=["rt"])
                    p.op("vector", lambda v: v.tensor_scalar(out=rt[:, 6:7], in0=rt[:, 5:6], scalar1=-1.0,
                                                             scalar2=1.0, op0=ALU.mult, op1=ALU.add),
                         reads=["rt"], writes=["rt"])
                    p.op("vector", lambda v: v.tensor_scalar(out=rt[:, 48:56], in0=rt[:, 16:24],
                                                             scalar1=rt[:, 6:7], scalar2=None, op0=ALU.mult),
                         reads=["rt"], writes=["rt"])
                    def mkw8(sub=sub):
                        return lambda v: v.scalar_tensor_tensor(out=w8tm[:, sub * 8:(sub + 1) * 8],
                                                                in0=rt[:, 32:40], scalar=rt[:, 5:6],
                                                                in1=rt[:, 48:56], op0=ALU.mult, op1=ALU.add)
                    p.op("vector", mkw8(), reads=["rt"], writes=["w8tm"])

                for e in range(E):
                    for sub in range(NSUB):
                        def mkex(e=e, sub=sub):
                            return lambda t: t.matmul(PD3[0:1, sub * 128:(sub + 1) * 128],
                                                      w8tm[:, sub * 8 + e: sub * 8 + e + 1], ident[:],
                                                      start=True, stop=True)
                        p.op("tensor", mkex(), reads=["w8tm", "ident"], writes=["PD3"])
                    p.op("scalar", lambda s: s.activation(outsb[0:1, 0:TT_TOK], PD3[0:1, 0:TT_TOK], AF.Identity),
                         reads=["PD3"], writes=["outsb"])
                    p.op("tensor", lambda t: t.matmul(PD2[:, 0:TT_TOK], ones[0:1, 0:128], outsb[0:1, 0:TT_TOK],
                                                      start=True, stop=True),
                         reads=["outsb", "ones"], writes=["PD2"])
                    def mkew(e=e):
                        return lambda v: v.tensor_copy(w8big[:, e * TT_TOK:(e + 1) * TT_TOK], PD2[:, 0:TT_TOK])
                    p.op("vector", mkew(), reads=["PD2"], writes=["w8big"])

                for (Asrc, hdst, an, hn) in ((A1Tg, h1sb, "ATa1", "h1sb"), (A3Tg, h3sb, "ATa3", "h3sb")):
                    for g in range(2):
                        for dc in range(8):
                            def mkh(Asrc=Asrc, g=g, dc=dc):
                                return lambda t: t.matmul(PD0[:, 0:TT_TOK], Asrc[g][:, dc * 128:(dc + 1) * 128],
                                                          xnfm[:, dc * TT_TOK:(dc + 1) * TT_TOK],
                                                          start=(dc == 0), stop=(dc == 7))
                            p.op("tensor", mkh(), reads=[f"{an}{g}", "xnfm"], writes=["PD0"])
                        def mkhe(hdst=hdst, g=g, hn=hn):
                            return lambda v: v.tensor_copy(hdst[:, g * TT_TOK:(g + 1) * TT_TOK], PD0[:, 0:TT_TOK])
                        p.op("vector", mkhe(), reads=["PD0"], writes=[hn])

                p.op("gpsimd", lambda g: g.memset(acctm[:, 0:NSUB * 1024], 0.0),
                     reads=["acctm"], writes=["acctm"])
                p.op("gpsimd", lambda g: g.memset(zsb[:], 0.0),
                     reads=["zsb"], writes=["zsb"])

                if tt == 0:
                    load_stripe(0, 0)
                for s in range(NS):
                    bi = s % 2
                    if s + 1 < NS:
                        load_stripe(s + 1, 1 - bi)
                    elif tt + 1 < NTT:
                        load_stripe(0, 1 - bi)
                    for (ws, wn, psl, psn) in ((w1s, "w1s", (PD0, PD1), ("PD0", "PD1")),
                                               (w3s, "w3s", (PD2, PD3), ("PD2", "PD3"))):
                        for m in range(2):
                            for dc in range(8):
                                def mkcm(ws=ws, bi=bi, m=m, dc=dc, ps=psl[m]):
                                    return lambda t: t.matmul(
                                        ps[:, 0:TT_TOK],
                                        ws[bi][:, dc * FS + m * 128: dc * FS + m * 128 + 128],
                                        xnfm[:, dc * TT_TOK:(dc + 1) * TT_TOK],
                                        start=(dc == 0), stop=(dc == 7))
                                p.op("tensor", mkcm(), reads=[f"{wn}{bi}", "xnfm"], writes=[psn[m]])
                    for m in range(2):
                        def mkec(m=m):
                            return lambda v: v.tensor_copy(c1sb[:, m * 512: m * 512 + TT_TOK],
                                                           (PD0, PD1)[m][:, 0:TT_TOK])
                        p.op("vector", mkec(), reads=[("PD0", "PD1")[m]], writes=["c1sb"])
                        def mkec3(m=m):
                            return lambda v: v.tensor_copy(c3sb[:, m * 512: m * 512 + TT_TOK],
                                                           (PD2, PD3)[m][:, 0:TT_TOK])
                        p.op("vector", mkec3(), reads=[("PD2", "PD3")[m]], writes=["c3sb"])

                    pending_z = []

                    def flush_z():
                        for fnz, rd, eidx in pending_z:
                            if fnz is None:
                                eidx()
                            else:
                                p.op("tensor", fnz, reads=rd, writes=["PZ"])
                        pending_z.clear()

                    for e in range(E):
                        g, el = divmod(e, 4)
                        tp = (el * 32, 0) if el else None
                        for m in range(2):
                            step = 2 * e + m
                            q2 = step % 2
                            q3 = step % 3
                            pd1 = (PD0, PD1)[q2]
                            pd3 = (PD2, PD3)[q2]
                            pn1 = ("PD0", "PD1")[q2]
                            pn3 = ("PD2", "PD3")[q2]
                            th = q3 * 512
                            tn1, tn3, tns = f"t1sb{q3}", f"t3sb{q3}", f"s1sb{q3}"
                            def mkd1(g=g, el=el, bi=bi, m=m, tp=tp, pd1=pd1):
                                return lambda t: t.matmul(
                                    pd1[:, 0:TT_TOK],
                                    b1s[bi][el * 32: el * 32 + 16, g * FS + m * 128: g * FS + m * 128 + 128],
                                    h1sb[el * 32: el * 32 + 16, g * TT_TOK:(g + 1) * TT_TOK],
                                    start=True, stop=True, tile_position=tp)
                            p.op("tensor", mkd1(), reads=[f"b1s{bi}", "h1sb"], writes=[pn1])
                            def mkd3(g=g, el=el, bi=bi, m=m, tp=tp, pd3=pd3):
                                return lambda t: t.matmul(
                                    pd3[:, 0:TT_TOK],
                                    b3s[bi][el * 32: el * 32 + 16, g * FS + m * 128: g * FS + m * 128 + 128],
                                    h3sb[el * 32: el * 32 + 16, g * TT_TOK:(g + 1) * TT_TOK],
                                    start=True, stop=True, tile_position=tp)
                            p.op("tensor", mkd3(), reads=[f"b3s{bi}", "h3sb"], writes=[pn3])
                            flush_z()
                            def mkt1(m=m, pd1=pd1, th=th):
                                return lambda v: v.scalar_tensor_tensor(
                                    out=t1sb[:, th: th + TT_TOK],
                                    in0=pd1[:, 0:TT_TOK], scalar=SCALE,
                                    in1=c1sb[:, m * 512: m * 512 + TT_TOK], op0=ALU.mult, op1=ALU.add)
                            p.op("vector", mkt1(), reads=[pn1, "c1sb"], writes=[tn1])
                            def mks1a(th=th):
                                return lambda s: s.activation(s1sb[:, th: th + TT_TOK],
                                                              t1sb[:, th: th + TT_TOK], AF.Silu)
                            p.op("scalar", mks1a(), reads=[tn1], writes=[tns])
                            def mkt3(m=m, pd3=pd3, th=th):
                                return lambda v: v.scalar_tensor_tensor(
                                    out=t3sb[:, th: th + TT_TOK],
                                    in0=pd3[:, 0:TT_TOK], scalar=SCALE,
                                    in1=c3sb[:, m * 512: m * 512 + TT_TOK], op0=ALU.mult, op1=ALU.add)
                            p.op("vector", mkt3(), reads=[pn3, "c3sb"], writes=[tn3])
                            def mkt3w(th=th, e=e):
                                return lambda v: v.tensor_tensor(
                                    out=t3sb[:, th: th + TT_TOK],
                                    in0=t3sb[:, th: th + TT_TOK],
                                    in1=w8big[:, e * TT_TOK:(e + 1) * TT_TOK], op=ALU.mult)
                            p.op("vector", mkt3w(), reads=[tn3, "w8big"], writes=[tn3])
                            def mkp(th=th):
                                return lambda v: v.tensor_tensor(
                                    out=t1sb[:, th: th + TT_TOK],
                                    in0=s1sb[:, th: th + TT_TOK],
                                    in1=t3sb[:, th: th + TT_TOK], op=ALU.mult)
                            p.op("vector", mkp(), reads=[tns, tn3], writes=[tn1])
                            if e == 0:
                                def mkacc(m=m, th=th):
                                    return lambda v: v.tensor_copy(accf[:, m * 512: m * 512 + TT_TOK],
                                                                   t1sb[:, th: th + TT_TOK])
                                p.op("vector", mkacc(), reads=[tn1], writes=["accf"])
                            else:
                                def mkacc(m=m, th=th):
                                    return lambda v: v.tensor_tensor(
                                        out=accf[:, m * 512: m * 512 + TT_TOK],
                                        in0=accf[:, m * 512: m * 512 + TT_TOK],
                                        in1=t1sb[:, th: th + TT_TOK], op=ALU.add)
                                p.op("vector", mkacc(), reads=[tn1, "accf"], writes=["accf"])
                            def mkz(m=m, g=g, el=el, bi=bi, th=th):
                                tpz = (0, el * 32) if el else None
                                return lambda t: t.matmul(
                                    PZ[el * 32: el * 32 + 32, 0:TT_TOK],
                                    a2s[bi][:, m * 256 + g * 128 + el * 32: m * 256 + g * 128 + el * 32 + 32],
                                    t1sb[:, th: th + TT_TOK],
                                    start=(m == 0), stop=(m == 1),
                                    tile_position=tpz)
                            pending_z.append((mkz(), [f"a2s{bi}", tn1], e))
                            if m == 1:
                                pass
                        def mkza_pending(g=g, el=el, e=e):
                            def emit_za():
                                def mkza():
                                    return lambda v: v.tensor_tensor(
                                        out=zsb[el * 32: el * 32 + 32, g * 512: g * 512 + TT_TOK],
                                        in0=zsb[el * 32: el * 32 + 32, g * 512: g * 512 + TT_TOK],
                                        in1=PZ[el * 32: el * 32 + 32, 0:TT_TOK], op=ALU.add)
                                p.op("vector", mkza(), reads=["zsb", "PZ"], writes=["zsb"])
                            return emit_za
                        pending_za = mkza_pending()
                        if e == E - 1:
                            flush_z()
                            pending_za()
                        else:
                            pending_z.append((None, None, pending_za))
                    for msub in range(NSUB):
                        for ch in range(FS // 128):
                            for nh in range(2):
                                def mkdn(msub=msub, ch=ch, nh=nh, bi=bi):
                                    return lambda t: t.matmul(
                                        PW[:, nh * 512:(nh + 1) * 512],
                                        accf[:, ch * 512 + msub * 128: ch * 512 + msub * 128 + 128],
                                        wds[bi][:, ch * 1024 + nh * 512: ch * 1024 + (nh + 1) * 512],
                                        start=(ch == 0), stop=(ch == FS // 128 - 1))
                                p.op("tensor", mkdn(), reads=["accf", f"wds{bi}"], writes=["PW"])
                        def mkda(msub=msub):
                            return lambda v: v.tensor_tensor(out=acctm[:, msub * 1024:(msub + 1) * 1024],
                                                             in0=acctm[:, msub * 1024:(msub + 1) * 1024],
                                                             in1=PW[:, 0:1024], op=ALU.add)
                        p.op("vector", mkda(), reads=["acctm", "PW"], writes=["acctm"])

                for msub in range(NSUB):
                    for nh in range(2):
                        for g in range(2):
                            def mko2m(msub=msub, nh=nh, g=g):
                                return lambda t: t.matmul(
                                    PW[:, nh * 512:(nh + 1) * 512],
                                    zsb[:, g * 512 + msub * 128: g * 512 + msub * 128 + 128],
                                    B2Tg[g][:, nh * 512:(nh + 1) * 512],
                                    start=(g == 0), stop=(g == 1))
                            p.op("tensor", mko2m(), reads=["zsb", f"B2T{g}"], writes=["PW"])
                    def mko2a(msub=msub):
                        return lambda v: v.scalar_tensor_tensor(
                            out=acctm[:, msub * 1024:(msub + 1) * 1024],
                            in0=PW[:, 0:1024], scalar=SCALE,
                            in1=acctm[:, msub * 1024:(msub + 1) * 1024], op0=ALU.mult, op1=ALU.add)
                    p.op("vector", mko2a(), reads=["acctm", "PW"], writes=["acctm"])

                for sub in range(NSUB):
                    src = acctm[:, sub * 1024:(sub + 1) * 1024]
                    p.op("scalar", lambda s, src=src: s.activation(t3sb[:, 0:1024], src, AF.Identity,
                                                                   accum_out=colsB[:, 8:9]),
                         reads=["acctm"], writes=["t3sb", "colsB"])
                    p.op("scalar", lambda s, src=src: s.activation(t3sb[:, 0:1024], src, AF.Square,
                                                                   accum_out=colsB[:, 9:10]),
                         reads=["acctm"], writes=["t3sb", "colsB"])
                    p.op("vector", lambda v: v.tensor_scalar(out=colsB[:, 8:9], in0=colsB[:, 8:9],
                                                             scalar1=1.0 / D, scalar2=None, op0=ALU.mult),
                         reads=["colsB"], writes=["colsB"])
                    p.op("scalar", lambda s: s.activation(colsB[:, 10:11], colsB[:, 8:9], AF.Square),
                         reads=["colsB"], writes=["colsB"])
                    p.op("vector", lambda v: v.scalar_tensor_tensor(out=colsB[:, 11:12], in0=colsB[:, 9:10],
                                                                    scalar=1.0 / D, in1=colsB[:, 10:11],
                                                                    op0=ALU.mult, op1=ALU.subtract),
                         reads=["colsB"], writes=["colsB"])
                    p.op("scalar", lambda s: s.activation(colsB[:, 12:13], colsB[:, 11:12], AF.Sqrt,
                                                          bias=LN_EPS),
                         reads=["colsB"], writes=["colsB"])
                    p.op("vector", lambda v: v.reciprocal(colsB[:, 12:13], colsB[:, 12:13]),
                         reads=["colsB"], writes=["colsB"])
                    p.op("vector", lambda v, src=src: v.tensor_scalar(out=outsb[:], in0=src,
                                                                      scalar1=colsB[:, 8:9],
                                                                      scalar2=colsB[:, 12:13],
                                                                      op0=ALU.subtract, op1=ALU.mult),
                         reads=["acctm", "colsB"], writes=["outsb"])
                    def mkfin(sub=sub):
                        return lambda v: v.tensor_tensor(out=outsb[:], in0=outsb[:],
                                                         in1=xpt[:, sub * 1024:(sub + 1) * 1024], op=ALU.add)
                    p.op("vector", mkfin(), reads=["outsb", "xpt"], writes=["outsb"])
                    p.dma("sync", y_out[t0 + sub * 128: t0 + (sub + 1) * 128, :], outsb[:],
                          reads=["outsb"], writes=["y_hbm"])

            p.emit(blkB, sems, start=mB, closer=True)
    return nc


def kernel(**inputs):
    x = np.ascontiguousarray(np.asarray(inputs["L0_attn_x"], dtype=np.float32))
    B = x.shape[0]

    cvals = np.broadcast_to(np.array(CONST_VALS, np.float32), (128, 8)).copy()
    ident = np.eye(128, dtype=np.float32)
    idsh = (np.eye(128, k=64) + np.eye(128, k=-64)).astype(np.float32)
    mask = np.where(np.arange(128)[None, :] >= np.arange(128)[:, None], 0.0, NEG).astype(np.float32)
    iota8 = np.broadcast_to(np.arange(8, dtype=np.float32), (128, 8)).copy()
    ones = np.ones((128, 128), dtype=np.float32)
    nw2 = np.tile(np.asarray(inputs["L0_attn_norm_w"], np.float32), 2).reshape(128, 1)

    common = {
        "Wq": np.ascontiguousarray(inputs["L0_attn_Wq"], np.float32),
        "Wk": np.ascontiguousarray(inputs["L0_attn_Wk"], np.float32),
        "Wv": np.ascontiguousarray(inputs["L0_attn_Wv"], np.float32),
        "Wconv": np.ascontiguousarray(inputs["L0_attn_Wconv"], np.float32),
        "Wa": np.ascontiguousarray(inputs["L0_attn_Wa"], np.float32),
        "Walpha": np.ascontiguousarray(inputs["L0_attn_Walpha"], np.float32),
        "Wb": np.ascontiguousarray(inputs["L0_attn_Wb"], np.float32),
        "A_log": np.ascontiguousarray(inputs["L0_attn_A_log"], np.float32).reshape(16, 1),
        "dt_bias": np.ascontiguousarray(inputs["L0_attn_dt_bias"], np.float32).reshape(16, 1),
        "norm_w2": nw2,
        "Wo": np.ascontiguousarray(inputs["L0_attn_Wo"], np.float32),
        "gate_w": np.ascontiguousarray(inputs["L0_ffn_gate_w"], np.float32),
        "w_gate": np.ascontiguousarray(inputs["L0_ffn_w_gate"], np.float32),
        "w_up": np.ascontiguousarray(inputs["L0_ffn_w_up"], np.float32),
        "w_down": np.ascontiguousarray(inputs["L0_ffn_w_down"], np.float32),
        "A1": np.ascontiguousarray(inputs["L0_ffn_A1"], np.float32).reshape(E * R, D),
        "B1": np.ascontiguousarray(inputs["L0_ffn_B1"], np.float32).reshape(E * F, R),
        "A3": np.ascontiguousarray(inputs["L0_ffn_A3"], np.float32).reshape(E * R, D),
        "B3": np.ascontiguousarray(inputs["L0_ffn_B3"], np.float32).reshape(E * F, R),
        "A2": np.ascontiguousarray(inputs["L0_ffn_A2"], np.float32).reshape(E * R, F),
        "B2": np.ascontiguousarray(inputs["L0_ffn_B2"], np.float32).reshape(E * D, R),
        "cvals": cvals, "ident": ident, "idsh": idsh, "maskc": mask, "iota8": iota8, "ones": ones,
    }

    nc = build_nc(Tloc=x.shape[1])
    in_maps = [dict(common, x=np.ascontiguousarray(x[b])) for b in range(B)]
    res = run_bass_kernel_spmd(nc, in_maps, core_ids=list(range(B)))
    out = np.stack([res.results[b]["y"] for b in range(B)], axis=0)
    return out.astype(np.float32)
